# version 09 (final)
# baseline (speedup 1.0000x reference)
import os
import sys

sys.path.insert(0, "/opt/trn_rl_repo")

PH = int(os.environ.get("KPH", "9"))
KDBG = int(os.environ.get("KDBG", "0"))
KSUB = int(os.environ.get("KSUB", "9"))

import numpy as np
import ml_dtypes

import concourse.bass as bass
import concourse.mybir as mybir
from concourse.bass_types_rust import DynSlice
from concourse.bass_utils import run_bass_kernel_spmd

F32 = mybir.dt.float32
BF16 = mybir.dt.bfloat16
I32 = mybir.dt.int32
AF = mybir.ActivationFunctionType
ALU = mybir.AluOpType
AX = mybir.AxisListType.X

S = 8192
E = 4096
NQH = 32
NKV = 8
NCORE = 8
OWN = 1024
HALF = 4096
QK_SCALE = 1.0 / float(np.sqrt(128.0))
EXP_BIAS = -20.0
LN_EPS = 1e-5
RQ0, RQ1 = 611, 8191
RCNT = RQ1 - RQ0
MAGIC = 123456789
KB = 1024


def build_nc():
    nc = bass.Bass("TRN2", target_bir_lowering=False)

    dx_half = nc.dram_tensor("x_half", [HALF, E], F32, kind="ExternalInput").ap()
    dx_own = nc.dram_tensor("x_own", [OWN, E], F32, kind="ExternalInput").ap()
    dw_q = nc.dram_tensor("w_qT", [E, 4096], F32, kind="ExternalInput").ap()
    dw_kv = nc.dram_tensor("w_kvT", [E, 1024], F32, kind="ExternalInput").ap()
    dw_o = nc.dram_tensor("w_oT", [E, E], F32, kind="ExternalInput").ap()
    dw_f = nc.dram_tensor("w_fT", [E, E], F32, kind="ExternalInput").ap()
    dA = nc.dram_tensor("a_T", [E, 128], F32, kind="ExternalInput").ap()
    dB = nc.dram_tensor("b_T", [128, E], F32, kind="ExternalInput").ap()
    dbias = nc.dram_tensor("fb", [1, E], F32, kind="ExternalInput").ap()
    drw = nc.dram_tensor("rw_T", [E, 8], F32, kind="ExternalInput").ap()
    drb = nc.dram_tensor("rb", [1, 8], F32, kind="ExternalInput").ap()
    dcos_own = nc.dram_tensor("cos_own", [64, OWN], F32, kind="ExternalInput").ap()
    dsin_own = nc.dram_tensor("sin_own", [64, OWN], F32, kind="ExternalInput").ap()
    dcos_all = nc.dram_tensor("cos_all", [64, S], F32, kind="ExternalInput").ap()
    dsin_all = nc.dram_tensor("sin_all", [64, S], F32, kind="ExternalInput").ap()
    dident = nc.dram_tensor("identity", [128, 128], BF16, kind="ExternalInput").ap()
    doct = nc.dram_tensor("octmask", [128, 8 * 128], BF16, kind="ExternalInput").ap()
    dones = nc.dram_tensor("onesrow", [1, 128], BF16, kind="ExternalInput").ap()
    dE = nc.dram_tensor("emat", [8, 128], BF16, kind="ExternalInput").ap()

    dout = nc.dram_tensor("out", [OWN, E], F32, kind="ExternalOutput").ap()
    if KDBG:
        dbg_xn = nc.dram_tensor("dbg_xn", [128, 32 * OWN], BF16, kind="ExternalOutput").ap()
        dbg_q = nc.dram_tensor("dbg_q", [128, OWN], BF16, kind="ExternalOutput").ap()
        dbg_kv = nc.dram_tensor("dbg_kv", [2048, 512], BF16, kind="ExternalOutput").ap()
        dbg_attn = nc.dram_tensor("dbg_attn", [128, 32 * OWN], BF16, kind="ExternalOutput").ap()
        dbg_s = nc.dram_tensor("dbg_s", [1024, 4096], BF16, kind="ExternalOutput").ap()
        dbg_r = nc.dram_tensor("dbg_r", [1, 48], F32, kind="ExternalOutput").ap()
        dbg_tl = nc.dram_tensor("dbg_tl", [128, 1024], BF16, kind="ExternalOutput").ap()
        dbg_racc = nc.dram_tensor("dbg_racc", [128, 32], F32, kind="ExternalOutput").ap()

    dQ = nc.dram_tensor("qscr", [NQH, 128, OWN], BF16, kind="Internal")
    dsS = nc.dram_tensor("sscr", [OWN, E], BF16, kind="Internal")
    dxn = nc.dram_tensor("xn_sh", [2, E, HALF], BF16, addr_space="Shared", kind="Internal")
    dKV = nc.dram_tensor("kv_sh", [2, 1024, S], BF16, addr_space="Shared", kind="Internal")
    dflg = nc.dram_tensor("flg_sh", [2, 32], I32, addr_space="Shared", kind="Internal")
    dxn_loc = nc.dram_tensor("xn_loc", [E, HALF], BF16, kind="Internal")
    dkv_loc = nc.dram_tensor("kv_loc", [1024, S], BF16, kind="Internal")
    dxn_own = nc.dram_tensor("xn_ownd", [E, OWN], BF16, kind="Internal")

    SB0 = 16512
    OFF = {"C": SB0, "R1": SB0 + 30 * KB, "R2": SB0 + 94 * KB, "R3": SB0 + 126 * KB, "R4": SB0 + 175 * KB}
    bump = {"C": 0}

    def at(region, name, shape, dtype, off=None):
        if off is None:
            o = OFF[region] + bump[region]
            per = int(np.prod(shape[1:])) * mybir.dt.size(dtype)
            bump[region] = bump[region] + ((per + 31) // 32) * 32
        else:
            o = OFF[region] + off
        return nc.alloc_sbuf_tensor_at(name, shape, dtype, offset=o)

    ident = at("C", "ident", [128, 128], BF16)
    octm = at("C", "octm", [128, 8 * 128], BF16)
    ones_r = at("C", "ones_r", [1, 128], BF16)
    ones_c = at("C", "ones_c", [128, 1], BF16)
    ones_cf = at("C", "ones_cf", [128, 1], F32)
    rl_f = at("C", "rl_f", [1, 512], F32)
    rl_sb = at("C", "rl_sb", [1, 512], BF16)
    lb_sb = at("C", "lb_sb", [128, 512], BF16)
    emat = at("C", "emat_sb", [8, 128], BF16)
    cpool = at("C", "cpool", [128, 8], F32)
    cos_own = at("C", "cos_own_sb", [64, OWN], F32)
    sin_own = at("C", "sin_own_sb", [64, OWN], F32)
    rtmp1 = at("C", "rtmp1", [64, 512], F32)
    rtmp2 = at("C", "rtmp2", [64, 512], F32)
    cosblk = at("C", "cosblk", [64, 2 * 256], F32)
    sinblk = at("C", "sinblk", [64, 2 * 256], F32)
    stats = at("C", "stats", [128, 16], F32)
    racc = at("C", "racc", [128, 32], F32)
    q_bf = at("C", "q_bf", [128, 32], BF16)
    rw_bf = at("C", "rw_bf", [128, 32 * 8], BF16)
    rb_bf = at("C", "rb_bf", [1, 8], BF16)
    rsm = at("C", "rsm", [1, 48], F32)
    rsm_bf = at("C", "rsm_bf", [1, 8], BF16)
    rcol_bf = at("C", "rcol_bf", [8, 1], BF16)
    r_er = at("C", "r_er", [128, 1], F32)
    t_lT = at("C", "t_lT", [128, 8 * 128], BF16)
    tltmp = at("C", "tltmp", [128, 2 * 128], BF16)
    flag_sb = at("C", "flag_sb", [1, 16], I32)
    zero_sb = at("C", "zero_sb", [1, 32], I32)
    pollbuf = at("C", "pollbuf", [1, 16], I32)
    rpart = at("C", "rpart", [128, 2], F32)
    assert bump["C"] <= 30 * KB, bump["C"]

    xn_own = at("R1", "xn_own", [128, 32 * OWN], BF16, off=0)
    attn_fm = at("R1", "attn_fm", [128, 32 * OWN], BF16, off=0)
    xn_blk = at("R2", "xn_blk", [128, 2, 32 * 256], BF16, off=0)
    rsw_in = at("R2", "rsw_in", [128, 2, 7680], BF16, off=0)
    qstage = at("R4", "qstage", [128, OWN], BF16, off=18 * KB)
    wslab = at("R2", "wslab", [128, 2, 32 * 256], BF16, off=0)
    xh = at("R3", "xh", [128, 2, 2048], F32, off=0)
    fmstage = at("R3", "fmstage", [128, 32 * 512], BF16, off=16 * KB)
    kvstr = at("R3", "kvstr", [128, 4 * 32 * 128], BF16, off=0)
    kvstage = at("R3", "kvstage", [128, 2, 4 * 256], BF16, off=32 * KB)
    qstr = at("R3", "qstr", [128, 2, 32 * 128], BF16, off=0)
    kg = at("R2", "kg", [128, 2, S], BF16, off=0)
    vfm = at("R3", "vfm", [128, 2, S], BF16, off=0)
    vp = at("R3", "vp", [128, 64 * 128], BF16, off=32 * KB)
    sstage = at("R3", "sstage", [128, 2, 8 * 256], BF16, off=0)
    a_bf = at("R3", "a_bf", [128, 32 * 128], BF16, off=0)
    b_bf = at("R3", "b_bf", [128, 4096], BF16, off=8 * KB)
    fb_bf = at("R3", "fb_bf", [1, 4096], BF16, off=16 * KB)
    mini = at("R4", "mini", [128, 2, 128], BF16, off=0)
    sqdump = at("R4", "sqdump", [128, 2048], F32, off=2 * KB)
    xnorm = at("R4", "xnorm", [128, 4096], BF16, off=10 * KB)
    ptbP = [at("R4", "ptb0", [128, 1536], BF16, off=0),
            at("C", "ptb1", [128, 1536], BF16, off=27 * KB)]
    accP = [at("R4", "accP0", [128, 256], F32, off=3 * KB),
            at("R3", "accP1", [128, 256], F32, off=48 * KB)]
    qg = at("R4", "qg", [128, 2, 4096], BF16, off=4 * KB)
    atmp = at("R4", "atmp", [128, 2 * 128], BF16, off=20 * KB)
    sdb = at("R4", "sdb", [128, 2, 8 * 256], BF16, off=0)
    xep = at("R4", "xep", [128, 2, 256], F32, off=8 * KB)
    outep = at("R4", "outep", [128, 2, 256], F32, off=10 * KB)
    gtile = at("R4", "gtile", [128, 2, 256], BF16, off=12 * KB)

    ps = [None] * 8

    CONSTS = [EXP_BIAS, QK_SCALE, LN_EPS]
    for k, v in enumerate(CONSTS):
        nc.const_aps.register(F32, v, cpool[:, k : k + 1])

    streams = {k: [] for k in ("pe", "act", "dve", "pool", "sp")}
    sems = {}
    cnt = {k: 0 for k in ("pe", "ac", "dv", "pl", "bar")}
    GR = {}
    SEMN = {"pe": "pe", "act": "ac", "dve": "dv", "pool": "pl", "sp": "pl"}

    def em(engn, fn):
        streams[engn].append(fn)

    def dma(qname, group, out, in_, **kw):
        GR[group] = GR.get(group, 0) + 16
        v = GR[group]

        def fn(g, out=out, in_=in_, kw=kw, group=group):
            g.dma_start(out=out, in_=in_, **kw).then_inc(sems[group], 16)

        em(qname, fn)
        return (group, v)

    def wait(engn, semn, val):
        if val <= 0:
            return
        em(engn, lambda g, semn=semn, val=val: g.wait_ge(sems[semn], val))

    def waitv(engn, tok):
        if tok is None:
            return
        wait(engn, tok[0], tok[1])

    def gtot(group):
        return (group, GR.get(group, 0))

    def tag(engn, instr_fn, semn=None):
        semn = semn or SEMN[engn]
        cnt[semn] += 1
        v = cnt[semn]

        def fn(g, instr_fn=instr_fn, semn=semn):
            instr_fn(g).then_inc(sems[semn], 1)

        em(engn, fn)
        return v

    def plain(engn, instr_fn):
        em(engn, lambda g, instr_fn=instr_fn: instr_fn(g))

    NBAR = [0]

    def barrier():
        NBAR[0] += 1
        tgt = 5 * NBAR[0]
        for engn in streams:
            def fn(g, tgt=tgt):
                g.nop().then_inc(sems["bar"], 1)
                g.wait_ge(sems["bar"], tgt)
            em(engn, fn)

    PV = {}

    def pvals(g):
        if "par" not in PV:
            pid = g.partition_id()
            par = g.snap(pid % 2)
            PV["par"] = par
            PV["oth"] = g.snap(1 - par)
            PV["p128"] = g.snap(pid * 128)
        return PV

    for _src, _dst in ((dident, ident), (doct, octm), (dones, ones_r), (dE, emat),
                       (dcos_own, cos_own), (dsin_own, sin_own)):
        dma("sp", "su", _dst[:, :], _src)
    dma("pool", "suP", rw_bf.ap().rearrange("p (k c) -> p k c", k=32),
        drw.rearrange("(k p) c -> p k c", p=128))
    dma("pool", "suP", rb_bf[:, :], drb)
    su_all = gtot("su")
    suP_all = gtot("suP")

    def f0(v):
        for k, val in enumerate(CONSTS):
            v.memset(cpool[:, k : k + 1], val)
        v.memset(racc[:, :], 0.0)
        v.memset(rpart[:, :], 0.0)
        v.memset(ones_c[:, :], 1.0)
        v.memset(ones_cf[:, :], 1.0)
        v.memset(flag_sb[:, :], MAGIC)
        return v.memset(zero_sb[:, :], 0)

    v_c0 = tag("dve", f0)

    def touch_out_and_stop():
        dma("sp", "touch", dout[:, :].rearrange("(t p) c -> p t c", p=128),
            dx_own.rearrange("(t p) c -> p t c", p=128))
        for grp in list(GR):
            q = "pool" if grp in ("suP", "fl", "kst", "kw0", "kw1", "cpx", "gat", "cpk",
                                  "pub", "qs0", "qs1", "wc0", "wc1", "dcon", "clr") else "sp"
            wait(q, grp, GR[grp])
        return

    fm_dma = []
    tp_of_k = {}
    ev_of_k = {}
    last_flush_ac = 0
    for t in range(32):
        win, sl = t // 4, t % 4
        xw = []
        for h in range(2):
            if t > 0:
                wait("sp", "dv", norm_done)
            xw.append(dma("sp", f"xh{h}", xh[:, h, :],
                          dx_half[t * 128 : (t + 1) * 128, h * 2048 : (h + 1) * 2048]))
        vacl = 0
        for h in range(2):
            waitv("dve", xw[h])
            plain("dve", lambda v, h=h: v.tensor_scalar(
                sqdump[:, :], xh[:, h, :], 1.0, None, ALU.mult, ALU.add,
                accum_out=stats[:, 0 + h : 1 + h]))
            vacl = tag("dve", lambda v, h=h: v.scalar_tensor_tensor(
                sqdump[:, :], xh[:, h, :], 1.0, xh[:, h, :], ALU.mult, ALU.mult,
                accum_out=stats[:, 2 + h : 3 + h]))
        wait("dve", "dv", vacl)
        v1 = tag("dve", lambda v: v.tensor_add(stats[:, 4:5], stats[:, 0:1], stats[:, 1:2]))
        wait("dve", "dv", v1)
        plain("dve", lambda v: v.tensor_add(stats[:, 5:6], stats[:, 2:3], stats[:, 3:4]))
        v2 = tag("dve", lambda v: v.tensor_scalar(stats[:, 4:5], stats[:, 4:5], 1.0 / E, None, ALU.mult))
        wait("dve", "dv", v2)
        v3 = tag("dve", lambda v: v.tensor_mul(stats[:, 8:9], stats[:, 4:5], stats[:, 4:5]))
        wait("dve", "dv", v3)
        v4 = tag("dve", lambda v: v.tensor_scalar(
            stats[:, 5:6], stats[:, 5:6], 1.0 / E, stats[:, 8:9], ALU.mult, ALU.subtract))
        wait("act", "dv", max(v4, v_c0))
        va = tag("act", lambda s: s.activation(stats[:, 6:7], stats[:, 5:6], AF.Sqrt, bias=LN_EPS))
        wait("dve", "ac", va)
        v5 = tag("dve", lambda v: v.reciprocal(stats[:, 7:8], stats[:, 6:7]))
        wait("dve", "dv", v5)
        if t > 0:
            wait("dve", "pe", last_tp)
        for h in range(2):
            vn = tag("dve", lambda v, h=h: v.tensor_scalar(
                xnorm[:, h * 2048 : (h + 1) * 2048], xh[:, h, :], stats[:, 4:5], stats[:, 7:8],
                ALU.subtract, ALU.mult))
        vnorm = vn
        wait("pe", "dv", vnorm)
        if t == 0:
            waitv("pe", su_all)
        for kb in range(0, 32, 4):
            pbank = 6 + (kb // 4) % 2
            if (t, kb - 8) in ev_of_k:
                wait("pe", "ac", ev_of_k[(t, kb - 8)])
            elif t > 0 and kb < 8:
                wait("pe", "ac", ev_of_k[(t - 1, 24 + kb)])
            for kk in range(4):
                k = kb + kk
                vtp = tag("pe", lambda tt, k=k, kk=kk, pbank=pbank: tt.matmul(
                    ps[pbank][:, kk * 128 : kk * 128 + 128], xnorm[:, k * 128 : (k + 1) * 128],
                    ident[:, :], start=True, stop=True))
            wait("act", "pe", vtp)
            if len(fm_dma) > 0 and kb == 0:
                waitv("act", fm_dma[-1])
            ev_of_k[(t, kb)] = tag("act", lambda s, pbank=pbank, kb=kb, sl=sl: s.copy(
                fmstage.ap().rearrange("p (k c) -> p k c", k=32)[:, kb : kb + 4, sl * 128 : sl * 128 + 128],
                ps[pbank].ap().rearrange("p (k c) -> p k c", k=4)))
        last_tp = cnt["pe"]
        norm_done = cnt["dv"]
        if sl == 3:
            wait("pool", "ac", cnt["ac"])

            fm_dma.append(dma("pool", "fl",
                dxn_loc.ap().rearrange("(k p) t -> p k t", p=128)[:, :, win * 512 : (win + 1) * 512],
                fmstage.ap().rearrange("p (k t) -> p k t", k=32)))

    waitv("pool", gtot("fl"))
    wait("pool", "dv", v_c0)
    GR["cpx"] = GR.get("cpx", 0) + 16

    def cpx1(g):
        pv = pvals(g)
        g.dma_start(out=dxn.ap()[DynSlice(pv["par"], 1), :, :],
                    in_=dxn_loc.ap()).then_inc(sems["cpx"], 16)

    em("pool", cpx1)
    waitv("pool", gtot("cpx"))

    def pub(word):
        def fn(g, word=word):
            pv = pvals(g)
            g.dma_start(out=dflg.ap()[DynSlice(pv["par"], 1), word * 16 : word * 16 + 16],
                        in_=flag_sb[:, :]).then_inc(sems["pub"], 16)
        GR["pub"] = GR.get("pub", 0) + 16
        em("pool", fn)
        waitv("pool", gtot("pub"))

    def poll(word, psem_name):
        def fn(g, word=word, psem_name=psem_name):
            pv = pvals(g)
            with g.register("fv" + psem_name) as fv, g.register("tgt" + psem_name) as tgt:
                g.reg_mov(fv, 0)
                g.reg_mov(tgt, 0)
                with g.While(lambda: g.compute_val(fv) != MAGIC):
                    g.reg_add(tgt, tgt, 16)
                    g.dma_start(out=pollbuf[:, :],
                                in_=dflg.ap()[DynSlice(pv["oth"], 1), word * 16 : word * 16 + 16]
                                ).then_inc(sems[psem_name], 16)
                    g.wait_ge(sems[psem_name], tgt)
                    g.reg_load(fv, pollbuf[0:1, 0:1])
        em("pool", fn)

    if PH < 1:
        touch_out_and_stop()
        return _emit(nc, sems, streams, ps, GR)
    pub(0)
    poll(0, "p1")
    gat_all = None
    v_xnrdy = tag("pool", lambda g: g.nop(), "pl")
    barrier()

    if PH < 2:
        touch_out_and_stop()
        return _emit(nc, sems, streams, ps, GR)
    kv_ev = {}
    pass_last_pe = 0
    for p_ in range(2 if KSUB >= 2 else 1):
        strw = []
        for c in range(4):
            if p_ == 1:
                wait("pool", "pe", pass_last_pe)
            strw.append(dma("pool", "kst",
                kvstr.ap().rearrange("p (c k e) -> p c k e", c=4, k=32)[:, c, :, :],
                dw_kv[:, p_ * 512 + c * 128 : p_ * 512 + (c + 1) * 128].rearrange("(k p) c -> p k c", p=128)))
        if p_ == 0:
            for j in range(8):
                def gat(g, j=j):
                    pv = pvals(g)
                    g.dma_start(out=dxn_own.ap()[:, j * 128 : (j + 1) * 128],
                                in_=dxn.ap()[j // 4][:, DynSlice(pv["p128"] + (j % 4) * 1024, 128)]).then_inc(sems["gat"], 16)
                GR["gat"] = GR.get("gat", 0) + 16
                em("pool", gat)
            gat_all = gtot("gat")
        for tb in range(32):
            par2 = tb % 2
            if tb >= 2:
                wait("sp", "pe", kv_ev[("mmlast", p_, tb - 2)])
            xv = dma("sp", f"xb{par2}", xn_blk[:, par2, :].rearrange("p (k t) -> p k t", k=32),
                     dxn.ap()[tb // 16].rearrange("(k p) t -> p k t", p=128)[:, :, (tb % 16) * 256 : (tb % 16 + 1) * 256])
            if p_ == 0:
                if tb >= 2:
                    sn2, vv2 = kv_ev[(p_, tb - 2, 3)]
                    wait("sp", sn2, vv2)
                dma("sp", f"cs{par2}", cosblk[:, par2 * 256 : (par2 + 1) * 256], dcos_all[:, tb * 256 : (tb + 1) * 256])
                cv = dma("sp", f"cs{par2}", sinblk[:, par2 * 256 : (par2 + 1) * 256], dsin_all[:, tb * 256 : (tb + 1) * 256])
            waitv("pe", xv)
            if tb == 0:
                waitv("pe", strw[3])
                wait("pe", "pl", v_xnrdy)
            for c in range(4):
                if (p_, tb - 1, c) in kv_ev:
                    sn, vv = kv_ev[(p_, tb - 1, c)]
                    wait("pe", sn, vv)
                for e in range(31):
                    plain("pe", lambda tt, c=c, e=e, par2=par2, st=(e == 0):
                          tt.matmul(ps[c][:, 0:256],
                                    kvstr[:, (c * 32 + e) * 128 : (c * 32 + e) * 128 + 128],
                                    xn_blk[:, par2, e * 256 : (e + 1) * 256],
                                    start=st, stop=False))
                vmm = tag("pe", lambda tt, c=c, par2=par2: tt.matmul(
                    ps[c][:, 0:256], kvstr[:, (c * 32 + 31) * 128 : (c * 32 + 31) * 128 + 128],
                    xn_blk[:, par2, 31 * 256 : 32 * 256], start=False, stop=True))
                if c == 3:
                    kv_ev[("mmlast", p_, tb)] = vmm
                if KSUB == 11:
                    kv_ev[(p_, tb, c)] = ("pe", vmm)
                    continue
                if p_ == 0 and KSUB == 12:
                    wait("act", "pe", vmm)
                    if ("w", p_, tb - 2) in kv_ev:
                        waitv("act", kv_ev[("w", p_, tb - 2)])
                    va = tag("act", lambda s_, c=c, par2=par2: s_.copy(kvstage[:, par2, c * 256 : (c + 1) * 256], ps[c][:, 0:256]))
                    kv_ev[(p_, tb, c)] = ("ac", va)
                elif p_ == 0:
                    wait("dve", "pe", vmm)
                    waitv("dve", cv)
                    if ("w", p_, tb - 2) in kv_ev:
                        waitv("dve", kv_ev[("w", p_, tb - 2)])
                    plain("dve", lambda v, c=c, par2=par2: v.tensor_mul(rtmp1[:, 0:256], ps[c][0:64, 0:256], cosblk[:, par2 * 256 : par2 * 256 + 256]))
                    plain("dve", lambda v, c=c, par2=par2: v.tensor_mul(rtmp2[:, 0:256], ps[c][64:128, 0:256], sinblk[:, par2 * 256 : par2 * 256 + 256]))
                    tag("dve", lambda v, c=c, par2=par2: v.tensor_sub(kvstage[0:64, par2, c * 256 : (c + 1) * 256], rtmp1[:, 0:256], rtmp2[:, 0:256]))
                    plain("dve", lambda v, c=c, par2=par2: v.tensor_mul(rtmp1[:, 0:256], ps[c][64:128, 0:256], cosblk[:, par2 * 256 : par2 * 256 + 256]))
                    plain("dve", lambda v, c=c, par2=par2: v.tensor_mul(rtmp2[:, 0:256], ps[c][0:64, 0:256], sinblk[:, par2 * 256 : par2 * 256 + 256]))
                    vr = tag("dve", lambda v, c=c, par2=par2: v.tensor_add(kvstage[64:128, par2, c * 256 : (c + 1) * 256], rtmp1[:, 0:256], rtmp2[:, 0:256]))
                    kv_ev[(p_, tb, c)] = ("dv", vr)
                else:
                    wait("act", "pe", vmm)
                    if ("w", p_, tb - 2) in kv_ev:
                        waitv("act", kv_ev[("w", p_, tb - 2)])
                    va = tag("act", lambda s_, c=c, par2=par2: s_.copy(kvstage[:, par2, c * 256 : (c + 1) * 256], ps[c][:, 0:256]))
                    kv_ev[(p_, tb, c)] = ("ac", va)
            if KSUB == 11 or KSUB == 13:
                continue
            sn_l, vl = kv_ev[(p_, tb, 3)]
            wait("pool", sn_l, vl)
            if p_ == 0:
                sn0, vl0 = kv_ev[(p_, tb, 0)]
                wait("pool", sn0, vl0)
            kv_ev[("w", p_, tb)] = dma("pool", f"kw{par2}",
                dkv_loc.ap()[p_ * 512 : (p_ + 1) * 512, :].rearrange(
                    "(h p) t -> p h t", p=128)[:, :, tb * 256 : (tb + 1) * 256],
                kvstage.ap()[:, par2, :].rearrange("p (h t) -> p h t", h=4))
        pass_last_pe = kv_ev[("mmlast", p_, 31)]
        waitv("pool", gtot("kw0"))
        waitv("pool", gtot("kw1"))
        if KSUB >= 3:
            GR["cpk"] = GR.get("cpk", 0) + 16

            def cp_kv_p(g, p_=p_):
                pv = pvals(g)
                g.dma_start(out=dKV.ap()[DynSlice(pv["par"], 1), p_ * 512 : (p_ + 1) * 512, :],
                            in_=dkv_loc.ap()[p_ * 512 : (p_ + 1) * 512, :]).then_inc(sems["cpk"], 16)

            em("pool", cp_kv_p)

    if KSUB >= 3:
        waitv("pool", gtot("cpk"))

    if KSUB >= 4:
        pub(1)
        poll(1, "p2")
    v_kvrdy = tag("pool", lambda g: g.nop(), "pl")
    barrier()

    if KSUB < 9:
        touch_out_and_stop()
        return _emit(nc, sems, streams, ps, GR)
    if PH < 3:
        touch_out_and_stop()
        return _emit(nc, sems, streams, ps, GR)
    waitv("sp", gat_all)
    v_xnown = dma("sp", "xo", xn_own.ap().rearrange("p (k c) -> p k c", k=32),
                  dxn_own.ap().rearrange("(k p) c -> p k c", p=128))

    rs_pl = {}

    def emit_sweep(k):
        pr = k % 2
        if k >= 2:
            wait("sp", "dv", rs_pl[k - 2])
        n0 = HALF - RQ0
        dma("sp", f"rs{pr}", rsw_in[:, pr, 0:n0], dxn.ap()[0, k * 128 : (k + 1) * 128, RQ0:HALF])
        rw_ = dma("sp", f"rs{pr}", rsw_in[:, pr, n0:RCNT], dxn.ap()[1, k * 128 : (k + 1) * 128, 0 : RQ1 - HALF])
        waitv("dve", rw_)
        vac = tag("dve", lambda v, pr=pr: v.reduce_sum(rpart[:, pr : pr + 1], rsw_in[:, pr, 0:RCNT], axis=AX))
        wait("dve", "dv", vac)
        vadd = tag("dve", lambda v, k=k, pr=pr: v.tensor_add(racc[:, k : k + 1], racc[:, k : k + 1], rpart[:, pr : pr + 1]))
        rs_pl[k] = vadd

    q_dma_w = []
    rope_dv = {}
    qmm_of = {}
    for qh in range(NQH):
        pq = qh % 2
        if qh >= 2:
            wait("pool", "pe", qmm_of[qh - 2])
        sw = dma("pool", f"qs{pq}", qstr[:, pq, :].rearrange("p (k c) -> p k c", k=32),
                 dw_q[:, qh * 128 : (qh + 1) * 128].rearrange("(k p) c -> p k c", p=128))
        for tb in range(2):
            bank = tb
            waitv("pe", sw)
            if qh == 0 and tb == 0:
                wait("pe", v_xnown[0], v_xnown[1])
            if (qh - 1, tb) in rope_dv:
                wait("pe", "dv", rope_dv[(qh - 1, tb)])
            for e in range(31):
                plain("pe", lambda tt, pq=pq, e=e, tb=tb, bank=bank, st=(e == 0):
                      tt.matmul(ps[bank][:, 0:512],
                                qstr[:, pq, e * 128 : (e + 1) * 128],
                                xn_own[:, e * 1024 + tb * 512 : e * 1024 + tb * 512 + 512],
                                start=st, stop=False))
            vmm = tag("pe", lambda tt, pq=pq, tb=tb, bank=bank: tt.matmul(
                ps[bank][:, 0:512], qstr[:, pq, 31 * 128 : 32 * 128],
                xn_own[:, 31 * 1024 + tb * 512 : 31 * 1024 + tb * 512 + 512], start=False, stop=True))
            if tb == 1:
                qmm_of[qh] = vmm
            wait("dve", "pe", vmm)
            if qh == 0 and tb == 0:
                waitv("dve", su_all)
            if qh >= 1:
                waitv("dve", q_dma_w[-1])
            plain("dve", lambda v, bank=bank, tb=tb: v.tensor_mul(rtmp1[:, :], ps[bank][0:64, 0:512], cos_own[:, tb * 512 : (tb + 1) * 512]))
            plain("dve", lambda v, bank=bank, tb=tb: v.tensor_mul(rtmp2[:, :], ps[bank][64:128, 0:512], sin_own[:, tb * 512 : (tb + 1) * 512]))
            tag("dve", lambda v, tb=tb: v.tensor_sub(qstage[0:64, tb * 512 : (tb + 1) * 512], rtmp1[:, :], rtmp2[:, :]))
            plain("dve", lambda v, bank=bank, tb=tb: v.tensor_mul(rtmp1[:, :], ps[bank][64:128, 0:512], cos_own[:, tb * 512 : (tb + 1) * 512]))
            plain("dve", lambda v, bank=bank, tb=tb: v.tensor_mul(rtmp2[:, :], ps[bank][0:64, 0:512], sin_own[:, tb * 512 : (tb + 1) * 512]))
            v2 = tag("dve", lambda v, tb=tb: v.tensor_add(qstage[64:128, tb * 512 : (tb + 1) * 512], rtmp1[:, :], rtmp2[:, :]))
            rope_dv[(qh, tb)] = v2
        wait("sp", "dv", rope_dv[(qh, 1)])
        qw = dma("sp", "qw", dQ.ap()[qh, :, :], qstage[:, :])
        q_dma_w.append(qw)
        emit_sweep(qh)
    v_qdone = q_dma_w[-1]
    v_sweep = cnt["dv"]
    barrier()

    if PH < 4:
        touch_out_and_stop()
        return _emit(nc, sems, streams, ps, GR)
    v_vpones = cnt["dv"]
    ol_guard = {}
    gend = {}
    kvq = {}

    def b_loads(x):
        px = x % 2
        krow = (x % 4) * 128
        vrow = 512 + (x % 4) * 128
        kvq[x] = (
            dma("sp", f"kg{px}", kg[:, px, :], dKV.ap()[x // 4, krow : krow + 128, :]),
            dma("sp", f"vf{px}", vfm[:, px, :], dKV.ap()[x // 4, vrow : vrow + 128, :]),
            dma("sp", f"qg{px}", qg[:, px, :].rearrange("p (h t) -> p h t", h=4),
                dQ.ap()[4 * x : 4 * x + 4, :, :].rearrange("h p t -> p h t")),
        )

    for g_ in range(NKV):
        pg = g_ % 2
        if g_ == 0:
            waitv("sp", gtot("qw"))
            b_loads(0)
        if g_ + 1 < NKV:
            if g_ >= 1:
                wait("sp", "pe", gend[g_ - 1])
            b_loads(g_ + 1)
        kw_, vw_, qgw = kvq[g_]
        waitv("pe", vw_)
        if g_ == 0:
            wait("pe", "pl", v_kvrdy)
            wait("pe", "dv", v_vpones)
        if g_ >= 1:
            wait("pe", "pe", gend[g_ - 1])
        vev = {}
        for b4 in range(0, 64, 4):
            pbank = (b4 // 4) % 2
            if b4 >= 8:
                wait("pe", "ac", vev[b4 - 8])
            for kk in range(4):
                vtp = tag("pe", lambda tt, b=b4 + kk, kk=kk, pg=pg, pbank=pbank: tt.matmul(
                    ps[pbank][:, kk * 128 : kk * 128 + 128], vfm[:, pg, b * 128 : (b + 1) * 128], ident[:, :], start=True, stop=True))
            wait("act", "pe", vtp)
            vev[b4] = tag("act", lambda s_, b4=b4, pbank=pbank: s_.copy(
                vp[:, b4 * 128 : b4 * 128 + 512], ps[pbank][:, 0:512]))
        v_vp = vev[60]
        waitv("pe", kw_)
        waitv("pe", qgw)
        wait("pe", "ac", v_vp)
        for qhl in range(4):
            qh = 4 * g_ + qhl
            for grp in range(2):
                j0 = grp * 4
                kmax = 32 if grp == 0 else 64
                batches = []
                m0 = 0
                while m0 < kmax:
                    batches.append(list(range(m0, min(m0 + 3, kmax))))
                    m0 += 3
                expB = {}
                maskB = {}
                pvB = {}
                addB = {}

                def emit_Sbatch(bi):
                    ms = batches[bi]
                    if bi >= 2:
                        wait("pe", "ac", expB[bi - 2])
                    if bi == 0 and "lb" in ol_guard:
                        wait("pe", "ac", ol_guard["lb"])
                    vS = None
                    for m in ms:
                        slot = (bi % 2) * 3 + (m - ms[0])
                        jlo = max(j0, m // 8)
                        rel = jlo - j0
                        ncols = (j0 + 4 - jlo) * 128
                        fn = (lambda tt, slot=slot, m=m, rel=rel, ncols=ncols, pg=pg, qhl=qhl, j0=j0:
                              tt.matmul(ps[slot][:, rel * 128 : rel * 128 + ncols],
                                        kg[:, pg, m * 128 : (m + 1) * 128],
                                        qg[:, pg, qhl * 1024 + j0 * 128 + rel * 128 : qhl * 1024 + j0 * 128 + rel * 128 + ncols],
                                        start=True, stop=True))
                        if m == ms[-1]:
                            vS = tag("pe", fn)
                        else:
                            plain("pe", fn)
                    wait("act", "pe", vS)
                    if bi >= 2:
                        wait("act", "pe", pvB[bi - 2])
                        wait("act", "dv", addB[bi - 2])
                    vE = None
                    for m in ms:
                        slot = (bi % 2) * 3 + (m - ms[0])
                        jlo = max(j0, m // 8)
                        rel = jlo - j0
                        ncols = (j0 + 4 - jlo) * 128
                        vE = tag("act", lambda s_, slot=slot, pp=(bi % 2), mi=(m - ms[0]), rel=rel, ncols=ncols: s_.activation(
                            ptbP[pp][:, mi * 512 + rel * 128 : mi * 512 + rel * 128 + ncols],
                            ps[slot][:, rel * 128 : rel * 128 + ncols],
                            AF.Exp, bias=EXP_BIAS, scale=QK_SCALE))
                    expB[bi] = vE
                    vM = None
                    for m in ms:
                        j_d = m // 8
                        if j0 <= j_d < j0 + 4:
                            if vM is None:
                                wait("dve", "ac", vE)
                            vM = tag("dve", lambda v, pp=(bi % 2), mi=(m - ms[0]), j_d=j_d, m=m, j0=j0: v.tensor_mul(
                                ptbP[pp][:, mi * 512 + (j_d - j0) * 128 : mi * 512 + (j_d - j0) * 128 + 128],
                                ptbP[pp][:, mi * 512 + (j_d - j0) * 128 : mi * 512 + (j_d - j0) * 128 + 128],
                                octm[:, (m % 8) * 128 : (m % 8) * 128 + 128]))
                    maskB[bi] = vM
                    if vM is None:
                        wait("dve", "ac", vE)
                    vA = None
                    for m in ms:
                        mi = m - ms[0]
                        jlo = max(j0, m // 8)
                        rel = jlo - j0
                        if m == 0 and "lmm" in ol_guard:
                            wait("dve", "pe", ol_guard["lmm"])
                        for hh in range(2):
                            lo = max(rel * 128, hh * 256)
                            hi = (hh + 1) * 256
                            if lo >= hi:
                                continue
                            if m == 0:
                                vA = tag("dve", lambda v, pp=(bi % 2), mi=mi, hh=hh, lo=lo, hi=hi: v.tensor_copy(
                                    accP[hh][:, lo - hh * 256 : hi - hh * 256],
                                    ptbP[pp][:, mi * 512 + lo : mi * 512 + hi]))
                            else:
                                vA = tag("dve", lambda v, pp=(bi % 2), mi=mi, hh=hh, lo=lo, hi=hi: v.tensor_add(
                                    accP[hh][:, lo - hh * 256 : hi - hh * 256],
                                    accP[hh][:, lo - hh * 256 : hi - hh * 256],
                                    ptbP[pp][:, mi * 512 + lo : mi * 512 + hi]))
                    addB[bi] = vA

                def emit_PVbatch(bi):
                    ms = batches[bi]
                    wait("pe", "ac", expB[bi])
                    if maskB[bi] is not None:
                        wait("pe", "dv", maskB[bi])
                    if bi == 0:
                        if "O" in ol_guard:
                            wait("pe", "dv", ol_guard["O"])
                    vl = None
                    for m in ms:
                        jlo = max(j0, m // 8)
                        rel = jlo - j0
                        ncols = (j0 + 4 - jlo) * 128
                        fn = (lambda tt, pp=(bi % 2), mi=(m - ms[0]), m=m, rel=rel, ncols=ncols, st=(m == 0):
                              tt.matmul(ps[6][:, rel * 128 : rel * 128 + ncols],
                                        vp[:, m * 128 : m * 128 + 128],
                                        ptbP[pp][:, mi * 512 + rel * 128 : mi * 512 + rel * 128 + ncols],
                                        start=st, stop=(m == kmax - 1)))
                        if m == ms[-1]:
                            vl = tag("pe", fn)
                        else:
                            plain("pe", fn)
                    pvB[bi] = vl

                emit_Sbatch(0)
                emit_Sbatch(1)
                for bi in range(2, len(batches)):
                    emit_PVbatch(bi - 2)
                    emit_Sbatch(bi)
                emit_PVbatch(len(batches) - 2)
                emit_PVbatch(len(batches) - 1)

                wait("pe", "dv", addB[len(batches) - 1])
                if "l" in ol_guard:
                    wait("pe", "dv", ol_guard["l"])
                plain("pe", lambda tt: tt.matmul(ps[7][0:1, 0:256], ones_cf[:, :], accP[0][:, :], start=True, stop=True))
                v_lmm = tag("pe", lambda tt: tt.matmul(ps[7][0:1, 256:512], ones_cf[:, :], accP[1][:, :], start=True, stop=True))
                ol_guard["lmm"] = v_lmm
                wait("dve", "pe", v_lmm)
                vrc0 = tag("dve", lambda v: v.reciprocal(rl_f[0:1, :], ps[7][0:1, 0:512]))
                ol_guard["l"] = vrc0
                wait("dve", "dv", vrc0)
                vrc = tag("dve", lambda v: v.tensor_copy(rl_sb[0:1, :], rl_f[0:1, :]))
                wait("pe", "dv", vrc)
                vbc = tag("pe", lambda tt: tt.matmul(ps[2][:, 0:512], ones_r[0:1, :], rl_sb[0:1, :], start=True, stop=True))
                wait("act", "pe", vbc)
                vlb = tag("act", lambda s_: s_.copy(lb_sb[:, :], ps[2][:, 0:512]))
                ol_guard["lb"] = vlb
                wait("dve", "ac", vlb)
                vtt = tag("dve", lambda v, qh=qh, grp=grp: v.tensor_mul(
                    attn_fm[:, qh * 1024 + grp * 512 : qh * 1024 + grp * 512 + 512],
                    ps[6][:, 0:512], lb_sb[:, :]))
                ol_guard["O"] = vtt
        gend[g_] = cnt["pe"]
    v_attn_done_dv = cnt["dv"]
    barrier()

    if PH < 5:
        touch_out_and_stop()
        return _emit(nc, sems, streams, ps, GR)
    c_pe_of = {}
    c_ev = {}
    for o in range(16):
        po = o % 2
        if o >= 2:
            wait("pool", "pe", c_pe_of[o - 2])
        ww = dma("pool", f"wc{po}", wslab[:, po, :].rearrange("p (k t) -> p k t", k=32),
                 dw_o[:, o * 256 : (o + 1) * 256].rearrange("(k p) t -> p k t", p=128))
        waitv("pe", ww)
        if o == 0:
            wait("pe", "dv", v_attn_done_dv)
        if o >= 1:
            wait("pe", "ac", c_ev[o - 1])
        for c in range(32):
            for t in range(8):
                mmfn = (lambda tt, c=c, t=t, po=po, st=(c == 0), sp_=(c == 31):
                        tt.matmul(ps[t][:, 0:256],
                                  attn_fm[:, c * 1024 + t * 128 : c * 1024 + t * 128 + 128],
                                  wslab[:, po, c * 256 : (c + 1) * 256],
                                  start=st, stop=sp_))
                if c == 31 and t == 7:
                    vmm = tag("pe", mmfn)
                else:
                    plain("pe", mmfn)
        c_pe_of[o] = vmm
        wait("act", "pe", vmm)
        if o >= 2:
            waitv("act", c_ev[("sdma", o - 2)])
        for t in range(8):
            vev2 = tag("act", lambda s_, t=t, po=po: s_.copy(sstage[:, po, t * 256 : (t + 1) * 256], ps[t][:, 0:256]))
        c_ev[o] = vev2
        wait("sp", "ac", vev2)
        sd = dma("sp", f"sd{po}", dsS.ap()[:, o * 256 : (o + 1) * 256].rearrange("(t p) c -> p t c", p=128),
                 sstage.ap()[:, po, :].rearrange("p (t c) -> p t c", t=8))
        c_ev[("sdma", o)] = sd
    v_s_done = c_ev[("sdma", 15)]
    if KDBG:
        wait("sp", "dv", v_attn_done_dv)
        dma("sp", "dbg", dbg_attn, attn_fm[:, :])
    barrier()

    if PH < 6:
        touch_out_and_stop()
        return _emit(nc, sems, streams, ps, GR)
    waitv("pool", gtot("sd0"))
    waitv("pool", gtot("sd1"))
    dma("pool", "dcon", a_bf.ap().rearrange("p (k c) -> p k c", k=32), dA.rearrange("(k p) c -> p k c", p=128))
    dma("pool", "dcon", b_bf[:, :], dB)
    dma("pool", "dcon", fb_bf[:, :], dbias)
    dcon_all = gtot("dcon")
    v_xo2 = dma("sp", "xo", xn_own.ap().rearrange("p (k c) -> p k c", k=32),
                dxn_own.ap().rearrange("(k p) c -> p k c", p=128))

    vqbf = tag("dve", lambda v: v.tensor_scalar(q_bf[:, :], racc[:, :], 1.0 / RCNT, None, ALU.mult))
    wait("pe", "dv", vqbf)
    waitv("pe", suP_all)
    for k in range(32):
        plain("pe", lambda tt, k=k, st=(k == 0): tt.matmul(
            ps[0][0:1, 0:8], q_bf[:, k : k + 1], rw_bf[:, k * 8 : (k + 1) * 8], start=st, stop=False))
    waitv("pe", su_all)
    vlg = tag("pe", lambda tt: tt.matmul(ps[0][0:1, 0:8], ones_r[0:1, 0:1], rb_bf[0:1, :], start=False, stop=True))
    wait("dve", "pe", vlg)
    vc1 = tag("dve", lambda v: v.tensor_copy(rsm[0:1, 0:8], ps[0][0:1, 0:8]))
    wait("dve", "dv", vc1)
    vc2 = tag("dve", lambda v: v.reduce_max(rsm[0:1, 8:9], rsm[0:1, 0:8], axis=AX))
    wait("dve", "dv", vc2)
    vc3 = tag("dve", lambda v: v.tensor_scalar(rsm[0:1, 9:10], rsm[0:1, 8:9], -1.0, None, ALU.mult))
    wait("act", "dv", vc3)
    vc4 = tag("act", lambda s_: s_.activation(rsm[0:1, 16:24], rsm[0:1, 0:8], AF.Exp, bias=rsm[0:1, 9:10]))
    wait("dve", "ac", vc4)
    vc5 = tag("dve", lambda v: v.reduce_sum(rsm[0:1, 10:11], rsm[0:1, 16:24], axis=AX))
    wait("dve", "dv", vc5)
    vc6 = tag("dve", lambda v: v.reciprocal(rsm[0:1, 11:12], rsm[0:1, 10:11]))
    wait("dve", "dv", vc6)
    vc7 = tag("dve", lambda v: v.tensor_scalar(rsm_bf[0:1, 0:8], rsm[0:1, 16:24], rsm[0:1, 11:12], 2.0, ALU.mult, ALU.mult))
    wait("pe", "dv", vc7)
    vc8 = tag("pe", lambda tt: tt.matmul(ps[1][0:8, 0:1], rsm_bf[0:1, 0:8], ones_r[0:1, 0:1], start=True, stop=True))
    wait("dve", "pe", vc8)
    vc9 = tag("dve", lambda v: v.tensor_copy(rcol_bf[:, :], ps[1][0:8, 0:1]))
    wait("pe", "dv", vc9)
    waitv("pe", su_all)
    vc10 = tag("pe", lambda tt: tt.matmul(ps[2][:, 0:1], emat[:, :], rcol_bf[:, :], start=True, stop=True))
    wait("dve", "pe", vc10)
    v_rer = tag("dve", lambda v: v.tensor_copy(r_er[:, :], ps[2][:, 0:1]))

    tl_g = {}
    for t in range(8):
        b1 = 3 + (t % 2)
        if t == 0:
            waitv("pe", v_xo2)
            waitv("pe", dcon_all)
        if ("ev", t - 2) in tl_g:
            wait("pe", "ac", tl_g[("ev", t - 2)])
        for e in range(31):
            plain("pe", lambda tt, e=e, t=t, b1=b1, st=(e == 0):
                  tt.matmul(ps[b1][:, 0:128],
                            xn_own[:, e * 1024 + t * 128 : e * 1024 + t * 128 + 128],
                            a_bf[:, e * 128 : (e + 1) * 128], start=st, stop=False))
        vmm = tag("pe", lambda tt, t=t, b1=b1: tt.matmul(
            ps[b1][:, 0:128], xn_own[:, 31 * 1024 + t * 128 : 31 * 1024 + t * 128 + 128],
            a_bf[:, 31 * 128 : 32 * 128], start=False, stop=True))
        wait("act", "pe", vmm)
        am = t % 2
        if ("tp", t - 2) in tl_g:
            wait("act", "pe", tl_g[("tp", t - 2)])
        vev3 = tag("act", lambda s_, b1=b1, am=am: s_.copy(tltmp[:, am * 128 : am * 128 + 128], ps[b1][:, 0:128]))
        tl_g[("ev", t)] = vev3
        wait("pe", "ac", vev3)
        b2 = 5 + (t % 2)
        if ("sc", t - 2) in tl_g:
            wait("pe", "dv", tl_g[("sc", t - 2)])
        vtp = tag("pe", lambda tt, am=am, b2=b2: tt.matmul(
            ps[b2][:, 0:128], tltmp[:, am * 128 : am * 128 + 128], ident[:, :], start=True, stop=True))
        tl_g[("tp", t)] = vtp
        wait("dve", "pe", vtp)
        if t == 0:
            wait("dve", "dv", v_rer)
        vsc = tag("dve", lambda v, b2=b2, t=t: v.tensor_scalar(
            t_lT[:, t * 128 : (t + 1) * 128], ps[b2][:, 0:128], r_er[:, 0:1], None, ALU.mult))
        tl_g[("sc", t)] = vsc
    v_tl_done = cnt["dv"]

    d_pe_of = {}
    sil_of = {}
    sg_of = {}
    out_dmas = []
    xep_guard = {}
    li = 0
    for o in range(16):
        po = o % 2
        if o >= 2:
            wait("pool", "pe", d_pe_of[o - 2])
            wait("sp", "dv", sg_of[(o - 2, 7)])
        ww = dma("pool", f"wc{po}", wslab[:, po, :].rearrange("p (k t) -> p k t", k=32),
                 dw_f[:, o * 256 : (o + 1) * 256].rearrange("(k p) t -> p k t", p=128))
        sv = dma("sp", f"sv{po}", sdb[:, po, :].rearrange("p (t c) -> p t c", t=8),
                 dsS.ap()[:, o * 256 : (o + 1) * 256].rearrange("(t p) c -> p t c", p=128))
        for t in range(8):
            waitv("pe", ww)
            if o == 0 and t == 0:
                waitv("pe", dcon_all)
                wait("pe", "dv", v_tl_done)
            if (o - 1, t) in sil_of:
                wait("pe", "ac", sil_of[(o - 1, t)])
            for e in range(32):
                plain("pe", lambda tt, e=e, t=t, po=po, st=(e == 0):
                      tt.matmul(ps[t][:, 0:256],
                                xn_own[:, e * 1024 + t * 128 : e * 1024 + t * 128 + 128],
                                wslab[:, po, e * 256 : (e + 1) * 256], start=st, stop=False))
            plain("pe", lambda tt, t=t, o=o: tt.matmul(
                ps[t][:, 0:256], ones_r[0:1, :], fb_bf[0:1, o * 256 : (o + 1) * 256], start=False, stop=False))
            vmm = tag("pe", lambda tt, t=t, o=o: tt.matmul(
                ps[t][:, 0:256], t_lT[:, t * 128 : (t + 1) * 128], b_bf[:, o * 256 : (o + 1) * 256], start=False, stop=True))
            if t == 7:
                d_pe_of[o] = vmm
            pg2 = li % 2
            wait("act", "pe", vmm)
            if li >= 2:
                wait("act", "dv", sg_of[((li - 2) // 8, (li - 2) % 8)])
            vsi = tag("act", lambda s_, t=t, pg2=pg2: s_.activation(gtile[:, pg2, :], ps[t][:, 0:256], AF.Silu))
            sil_of[(o, t)] = vsi
            xv = dma("sp", f"xe{li % 2}", xep[:, li % 2, :], dx_own[t * 128 : (t + 1) * 128, o * 256 : (o + 1) * 256])
            wait("dve", "ac", vsi)
            waitv("dve", sv)
            if len(out_dmas) >= 2:
                waitv("dve", out_dmas[-2])
            vsg = tag("dve", lambda v, t=t, po=po, pg2=pg2, li=li: v.tensor_mul(
                outep[:, li % 2, :], sdb[:, po, t * 256 : (t + 1) * 256], gtile[:, pg2, :]))
            sg_of[(o, t)] = vsg
            waitv("dve", xv)
            wait("dve", "dv", vsg)
            vout = tag("dve", lambda v, li=li: v.tensor_add(outep[:, li % 2, :], outep[:, li % 2, :], xep[:, li % 2, :]))
            sg_of[(o, t)] = vout
            wait("sp", "dv", vout)
            ov = dma("sp", f"ov{li % 2}", dout[t * 128 : (t + 1) * 128, o * 256 : (o + 1) * 256], outep[:, li % 2, :])
            out_dmas.append(ov)
            li += 1

    if KDBG:
        wait("sp", "dv", cnt["dv"])
        wait("sp", "ac", cnt["ac"])
        dma("sp", "dbg", dbg_xn, xn_own[:, :])
        dma("sp", "dbg", dbg_q, dQ.ap()[0, :, :])
        dma("sp", "dbg", dbg_kv, dKV.ap().rearrange("a b t -> (a b) t")[:, 0:512])
        dma("sp", "dbg", dbg_s, dsS.ap()[:, :])
        dma("sp", "dbg", dbg_r, rsm[:, :])
        dma("sp", "dbg", dbg_tl, t_lT[:, :])
        dma("sp", "dbg", dbg_racc, racc[:, :])

    def clr(g):
        pv = pvals(g)
        g.dma_start(out=dflg.ap()[DynSlice(pv["oth"], 1), 0:32], in_=zero_sb[:, :]).then_inc(sems["clr"], 16)

    GR["clr"] = GR.get("clr", 0) + 16
    em("pool", clr)
    POOLG = ("suP", "fl", "kst", "kw0", "kw1", "cpx", "gat", "cpk", "pub",
             "qs0", "qs1", "wc0", "wc1", "dcon", "clr")
    for grp in list(GR):
        q = "pool" if grp in POOLG else "sp"
        wait(q, grp, GR[grp])

    return _emit(nc, sems, streams, ps, GR)


def _emit(nc, sems, streams, ps, GR):
    from contextlib import ExitStack
    with ExitStack() as ctx:
        e = ctx.enter_context
        for sname in ("pe", "ac", "dv", "pl", "bar", "p1", "p2"):
            sems[sname] = e(nc.semaphore(sname))
        for grp in GR:
            sems[grp] = e(nc.semaphore("g_" + grp))
        for i in range(8):
            ps[i] = e(nc.psum_tensor(f"ps{i}", [128, 512], F32))
        block = e(nc.Block())

        @block.tensor
        def _(t):
            for fn in streams["pe"]:
                fn(t)

        @block.scalar
        def _(s):
            for fn in streams["act"]:
                fn(s)

        @block.vector
        def _(v):
            for fn in streams["dve"]:
                fn(v)

        @block.gpsimd
        def _(g):
            for fn in streams["pool"]:
                fn(g)

        @block.sync
        def _(g):
            for fn in streams["sp"]:
                fn(g)

    return nc



def _prep_inputs(L0_attn_x, L0_attn_w_in, L0_attn_w_out, L0_ffn_W, L0_ffn_b,
                 L0_ffn_A, L0_ffn_B, L0_ffn_router_W, L0_ffn_router_b):
    x = np.asarray(L0_attn_x).reshape(S, E)
    w_in = np.asarray(L0_attn_w_in)
    w_out = np.asarray(L0_attn_w_out)
    W = np.asarray(L0_ffn_W)
    fb = np.ascontiguousarray(np.asarray(L0_ffn_b).reshape(1, E))
    A = np.asarray(L0_ffn_A)
    B = np.asarray(L0_ffn_B)
    rW = np.asarray(L0_ffn_router_W)
    rb = np.ascontiguousarray(np.asarray(L0_ffn_router_b).reshape(1, 8))

    w_qT = np.ascontiguousarray(w_in[0:4096].T)
    w_oT = np.ascontiguousarray(w_out.T)
    w_fT = np.ascontiguousarray(W.T)
    a_T = np.ascontiguousarray(A.transpose(2, 0, 1).reshape(E, 128))
    b_T = np.ascontiguousarray(B.transpose(0, 2, 1).reshape(128, E))
    rw_T = np.ascontiguousarray(rW.T)

    inv = 1.0 / (10000.0 ** (np.arange(0, 64, dtype=np.float64) * 2.0 / 128.0))
    pos = np.arange(S, dtype=np.float64)
    ang = pos[None, :] * inv[:, None]
    cos_all = np.ascontiguousarray(np.cos(ang).astype(np.float32))
    sin_all = np.ascontiguousarray(np.sin(ang).astype(np.float32))

    ident = np.eye(128, dtype=ml_dtypes.bfloat16)
    ones_r = np.ones((1, 128), dtype=ml_dtypes.bfloat16)
    emat = np.zeros((8, 128), dtype=np.float32)
    for e_ in range(8):
        emat[e_, e_ * 16 : (e_ + 1) * 16] = 1.0
    emat = emat.astype(ml_dtypes.bfloat16)

    in_maps = []
    for i in range(NCORE):
        par = i % 2
        own_rows = ((8 * np.arange(8)[:, None] + i) * 128 + np.arange(128)[None, :]).reshape(-1)
        oct_i = np.zeros((128, 8 * 128), dtype=np.float32)
        for o in range(8):
            if o < i:
                oct_i[:, o * 128 : (o + 1) * 128] = 1.0
            elif o == i:
                oct_i[:, o * 128 : (o + 1) * 128] = np.triu(np.ones((128, 128), dtype=np.float32)).T.T * 0 + np.triu(np.ones((128, 128), dtype=np.float32))
        oct_i = oct_i.astype(ml_dtypes.bfloat16)
        krows = w_in[4096 + par * 512 : 4096 + (par + 1) * 512]
        vrows = w_in[5120 + par * 512 : 5120 + (par + 1) * 512]
        w_kvT = np.ascontiguousarray(np.concatenate([krows, vrows], axis=0).T)
        in_maps.append({
            "x_half": np.ascontiguousarray(x[par * HALF : (par + 1) * HALF]),
            "x_own": np.ascontiguousarray(x[own_rows]),
            "w_qT": w_qT, "w_kvT": w_kvT, "w_oT": w_oT, "w_fT": w_fT,
            "a_T": a_T, "b_T": b_T, "fb": fb, "rw_T": rw_T, "rb": rb,
            "cos_own": np.ascontiguousarray(cos_all[:, own_rows]),
            "sin_own": np.ascontiguousarray(sin_all[:, own_rows]),
            "cos_all": cos_all, "sin_all": sin_all,
            "identity": ident, "octmask": oct_i, "onesrow": ones_r, "emat": emat,
        })
    return in_maps


def kernel(**inputs):
    in_maps = _prep_inputs(**inputs)
    nc = build_nc()
    res = run_bass_kernel_spmd(nc, in_maps, core_ids=list(range(NCORE)))
    out = np.zeros((S, E), dtype=np.float32)
    for i in range(NCORE):
        own_rows = ((8 * np.arange(8)[:, None] + i) * 128 + np.arange(128)[None, :]).reshape(-1)
        out[own_rows] = res.results[i]["out"]
    return out.reshape(1, S, E)


# version 10
# speedup vs baseline: 1.3646x; 1.3646x over previous
import os
import sys

sys.path.insert(0, "/opt/trn_rl_repo")

PH = int(os.environ.get("KPH", "9"))
KDBG = int(os.environ.get("KDBG", "0"))
KSUB = int(os.environ.get("KSUB", "9"))

import numpy as np
import ml_dtypes

import concourse.bass as bass
import concourse.mybir as mybir
from concourse.bass_types_rust import DynSlice
from concourse.bass_utils import run_bass_kernel_spmd

F32 = mybir.dt.float32
BF16 = mybir.dt.bfloat16
I32 = mybir.dt.int32
AF = mybir.ActivationFunctionType
ALU = mybir.AluOpType
AX = mybir.AxisListType.X

S = 8192
E = 4096
NQH = 32
NKV = 8
NCORE = 8
OWN = 1024
HALF = 4096
QK_SCALE = 1.0 / float(np.sqrt(128.0))
EXP_BIAS = -20.0
LN_EPS = 1e-5
RQ0, RQ1 = 611, 8191
RCNT = RQ1 - RQ0
MAGIC = 123456789
KB = 1024


def build_nc():
    nc = bass.Bass("TRN2", target_bir_lowering=False)

    dx_half = nc.dram_tensor("x_half", [HALF, E], F32, kind="ExternalInput").ap()
    dx_own = nc.dram_tensor("x_own", [OWN, E], F32, kind="ExternalInput").ap()
    dw_q = nc.dram_tensor("w_qT", [E, 4096], F32, kind="ExternalInput").ap()
    dw_kv = nc.dram_tensor("w_kvT", [E, 1024], F32, kind="ExternalInput").ap()
    dw_o = nc.dram_tensor("w_oT", [E, E], F32, kind="ExternalInput").ap()
    dw_f = nc.dram_tensor("w_fT", [E, E], F32, kind="ExternalInput").ap()
    dA = nc.dram_tensor("a_T", [E, 128], F32, kind="ExternalInput").ap()
    dB = nc.dram_tensor("b_T", [128, E], F32, kind="ExternalInput").ap()
    dbias = nc.dram_tensor("fb", [1, E], F32, kind="ExternalInput").ap()
    drw = nc.dram_tensor("rw_T", [E, 8], F32, kind="ExternalInput").ap()
    drb = nc.dram_tensor("rb", [1, 8], F32, kind="ExternalInput").ap()
    dcos_own = nc.dram_tensor("cos_own", [64, OWN], F32, kind="ExternalInput").ap()
    dsin_own = nc.dram_tensor("sin_own", [64, OWN], F32, kind="ExternalInput").ap()
    dcos_all = nc.dram_tensor("cos_all", [64, S], F32, kind="ExternalInput").ap()
    dsin_all = nc.dram_tensor("sin_all", [64, S], F32, kind="ExternalInput").ap()
    dident = nc.dram_tensor("identity", [128, 128], BF16, kind="ExternalInput").ap()
    doct = nc.dram_tensor("octmask", [128, 8 * 128], BF16, kind="ExternalInput").ap()
    dones = nc.dram_tensor("onesrow", [1, 128], BF16, kind="ExternalInput").ap()
    dE = nc.dram_tensor("emat", [8, 128], BF16, kind="ExternalInput").ap()

    dout = nc.dram_tensor("out", [OWN, E], F32, kind="ExternalOutput").ap()
    if KDBG:
        dbg_xn = nc.dram_tensor("dbg_xn", [128, 32 * OWN], BF16, kind="ExternalOutput").ap()
        dbg_q = nc.dram_tensor("dbg_q", [128, OWN], BF16, kind="ExternalOutput").ap()
        dbg_kv = nc.dram_tensor("dbg_kv", [2048, 512], BF16, kind="ExternalOutput").ap()
        dbg_attn = nc.dram_tensor("dbg_attn", [128, 32 * OWN], BF16, kind="ExternalOutput").ap()
        dbg_s = nc.dram_tensor("dbg_s", [1024, 4096], BF16, kind="ExternalOutput").ap()
        dbg_r = nc.dram_tensor("dbg_r", [1, 48], F32, kind="ExternalOutput").ap()
        dbg_tl = nc.dram_tensor("dbg_tl", [128, 1024], BF16, kind="ExternalOutput").ap()
        dbg_racc = nc.dram_tensor("dbg_racc", [128, 32], F32, kind="ExternalOutput").ap()

    dQ = nc.dram_tensor("qscr", [NQH, 128, OWN], BF16, kind="Internal")
    dsS = nc.dram_tensor("sscr", [OWN, E], BF16, kind="Internal")
    dxn = nc.dram_tensor("xn_sh", [2, E, HALF], BF16, addr_space="Shared", kind="Internal")
    dKV = nc.dram_tensor("kv_sh", [2, 1024, S], BF16, addr_space="Shared", kind="Internal")
    dflg = nc.dram_tensor("flg_sh", [2, 32], I32, addr_space="Shared", kind="Internal")
    dxn_loc = nc.dram_tensor("xn_loc", [E, HALF], BF16, kind="Internal")
    dkv_loc = nc.dram_tensor("kv_loc", [1024, S], BF16, kind="Internal")
    dxn_own = nc.dram_tensor("xn_ownd", [E, OWN], BF16, kind="Internal")

    SB0 = 16512
    OFF = {"C": SB0, "R1": SB0 + 30 * KB, "R2": SB0 + 94 * KB, "R3": SB0 + 126 * KB, "R4": SB0 + 175 * KB}
    bump = {"C": 0}

    def at(region, name, shape, dtype, off=None):
        if off is None:
            o = OFF[region] + bump[region]
            per = int(np.prod(shape[1:])) * mybir.dt.size(dtype)
            bump[region] = bump[region] + ((per + 31) // 32) * 32
        else:
            o = OFF[region] + off
        return nc.alloc_sbuf_tensor_at(name, shape, dtype, offset=o)

    ident = at("C", "ident", [128, 128], BF16)
    octm = at("C", "octm", [128, 8 * 128], BF16)
    ones_r = at("C", "ones_r", [1, 128], BF16)
    ones_c = at("C", "ones_c", [128, 1], BF16)
    ones_cf = at("C", "ones_cf", [128, 1], F32)
    rl_f = at("C", "rl_f", [1, 512], F32)
    rl_sb = at("C", "rl_sb", [1, 512], BF16)
    lb_sb = at("C", "lb_sb", [128, 512], BF16)
    emat = at("C", "emat_sb", [8, 128], BF16)
    cpool = at("C", "cpool", [128, 8], F32)
    cos_own = at("C", "cos_own_sb", [64, OWN], F32)
    sin_own = at("C", "sin_own_sb", [64, OWN], F32)
    rtmp1 = at("C", "rtmp1", [64, 512], F32)
    rtmp2 = at("C", "rtmp2", [64, 512], F32)
    cosblk = at("C", "cosblk", [64, 2 * 256], F32)
    sinblk = at("C", "sinblk", [64, 2 * 256], F32)
    stats = at("C", "stats", [128, 16], F32)
    racc = at("C", "racc", [128, 32], F32)
    q_bf = at("C", "q_bf", [128, 32], BF16)
    rw_bf = at("C", "rw_bf", [128, 32 * 8], BF16)
    rb_bf = at("C", "rb_bf", [1, 8], BF16)
    rsm = at("C", "rsm", [1, 48], F32)
    rsm_bf = at("C", "rsm_bf", [1, 8], BF16)
    rcol_bf = at("C", "rcol_bf", [8, 1], BF16)
    r_er = at("C", "r_er", [128, 1], F32)
    t_lT = at("C", "t_lT", [128, 8 * 128], BF16)
    tltmp = at("C", "tltmp", [128, 2 * 128], BF16)
    flag_sb = at("C", "flag_sb", [1, 16], I32)
    zero_sb = at("C", "zero_sb", [1, 32], I32)
    pollbuf = at("C", "pollbuf", [1, 16], I32)
    rpart = at("C", "rpart", [128, 2], F32)
    assert bump["C"] <= 30 * KB, bump["C"]

    xn_own = at("R1", "xn_own", [128, 32 * OWN], BF16, off=0)
    attn_fm = at("R1", "attn_fm", [128, 32 * OWN], BF16, off=0)
    xn_blk = at("R2", "xn_blk", [128, 2, 32 * 256], BF16, off=0)
    rsw_in = at("R2", "rsw_in", [128, 2, 7680], BF16, off=0)
    qstage = at("R4", "qstage", [128, OWN], BF16, off=18 * KB)
    wslab = at("R2", "wslab", [128, 2, 32 * 256], BF16, off=0)
    xh = at("R3", "xh", [128, 2, 2048], F32, off=0)
    fmstage = at("R3", "fmstage", [128, 32 * 512], BF16, off=16 * KB)
    kvstr = at("R3", "kvstr", [128, 4 * 32 * 128], BF16, off=0)
    kvstage = at("R3", "kvstage", [128, 2, 4 * 256], BF16, off=32 * KB)
    qstr = at("R3", "qstr", [128, 2, 32 * 128], BF16, off=0)
    kg = at("R2", "kg", [128, 2, S], BF16, off=0)
    vfm = at("R3", "vfm", [128, 2, S], BF16, off=0)
    vp = at("R3", "vp", [128, 64 * 128], BF16, off=32 * KB)
    sstage = at("R3", "sstage", [128, 2, 8 * 256], BF16, off=0)
    a_bf = at("R3", "a_bf", [128, 32 * 128], BF16, off=0)
    b_bf = at("R3", "b_bf", [128, 4096], BF16, off=8 * KB)
    fb_bf = at("R3", "fb_bf", [1, 4096], BF16, off=16 * KB)
    mini = at("R4", "mini", [128, 2, 128], BF16, off=0)
    sqdump = at("R4", "sqdump", [128, 2048], F32, off=2 * KB)
    xnorm = at("R4", "xnorm", [128, 4096], BF16, off=10 * KB)
    ptbP = [at("R4", "ptb0", [128, 1536], BF16, off=0),
            at("C", "ptb1", [128, 1536], BF16, off=27 * KB)]
    accP = [at("R4", "accP0", [128, 256], F32, off=3 * KB),
            at("R3", "accP1", [128, 256], F32, off=48 * KB)]
    qg = at("R4", "qg", [128, 2, 4096], BF16, off=4 * KB)
    atmp = at("R4", "atmp", [128, 2 * 128], BF16, off=20 * KB)
    sdb = at("R4", "sdb", [128, 2, 8 * 256], BF16, off=0)
    xep = at("R4", "xep", [128, 2, 256], F32, off=8 * KB)
    outep = at("R4", "outep", [128, 2, 256], F32, off=10 * KB)
    gtile = at("R4", "gtile", [128, 2, 256], BF16, off=12 * KB)

    ps = [None] * 8

    CONSTS = [EXP_BIAS, QK_SCALE, LN_EPS]
    for k, v in enumerate(CONSTS):
        nc.const_aps.register(F32, v, cpool[:, k : k + 1])

    streams = {k: [] for k in ("pe", "act", "dve", "pool", "sp")}
    sems = {}
    cnt = {k: 0 for k in ("pe", "ac", "dv", "pl", "bar")}
    GR = {}
    SEMN = {"pe": "pe", "act": "ac", "dve": "dv", "pool": "pl", "sp": "pl"}

    def em(engn, fn):
        streams[engn].append(fn)

    def dma(qname, group, out, in_, **kw):
        GR[group] = GR.get(group, 0) + 16
        v = GR[group]

        def fn(g, out=out, in_=in_, kw=kw, group=group):
            g.dma_start(out=out, in_=in_, **kw).then_inc(sems[group], 16)

        em(qname, fn)
        return (group, v)

    def wait(engn, semn, val):
        if val <= 0:
            return
        em(engn, lambda g, semn=semn, val=val: g.wait_ge(sems[semn], val))

    def waitv(engn, tok):
        if tok is None:
            return
        wait(engn, tok[0], tok[1])

    def gtot(group):
        return (group, GR.get(group, 0))

    def tag(engn, instr_fn, semn=None):
        semn = semn or SEMN[engn]
        cnt[semn] += 1
        v = cnt[semn]

        def fn(g, instr_fn=instr_fn, semn=semn):
            instr_fn(g).then_inc(sems[semn], 1)

        em(engn, fn)
        return v

    def plain(engn, instr_fn):
        em(engn, lambda g, instr_fn=instr_fn: instr_fn(g))

    NBAR = [0]

    def barrier():
        NBAR[0] += 1
        tgt = 5 * NBAR[0]
        for engn in streams:
            def fn(g, tgt=tgt):
                g.nop().then_inc(sems["bar"], 1)
                g.wait_ge(sems["bar"], tgt)
            em(engn, fn)

    PV = {}

    def pvals(g):
        if "par" not in PV:
            pid = g.partition_id()
            par = g.snap(pid % 2)
            PV["par"] = par
            PV["oth"] = g.snap(1 - par)
            PV["p128"] = g.snap(pid * 128)
        return PV

    for _src, _dst in ((dident, ident), (doct, octm), (dones, ones_r), (dE, emat),
                       (dcos_own, cos_own), (dsin_own, sin_own)):
        dma("sp", "su", _dst[:, :], _src)
    dma("pool", "suP", rw_bf.ap().rearrange("p (k c) -> p k c", k=32),
        drw.rearrange("(k p) c -> p k c", p=128))
    dma("pool", "suP", rb_bf[:, :], drb)
    su_all = gtot("su")
    suP_all = gtot("suP")

    def f0(v):
        for k, val in enumerate(CONSTS):
            v.memset(cpool[:, k : k + 1], val)
        v.memset(racc[:, :], 0.0)
        v.memset(rpart[:, :], 0.0)
        v.memset(ones_c[:, :], 1.0)
        v.memset(ones_cf[:, :], 1.0)
        v.memset(flag_sb[:, :], MAGIC)
        return v.memset(zero_sb[:, :], 0)

    v_c0 = tag("dve", f0)

    def touch_out_and_stop():
        dma("sp", "touch", dout[:, :].rearrange("(t p) c -> p t c", p=128),
            dx_own.rearrange("(t p) c -> p t c", p=128))
        for grp in list(GR):
            q = "pool" if grp in ("suP", "fl", "kst", "kw0", "kw1", "cpx", "gat", "cpk",
                                  "pub", "qs0", "qs1", "wc0", "wc1", "dcon", "clr") else "sp"
            wait(q, grp, GR[grp])
        return

    fm_dma = []
    tp_of_k = {}
    ev_of_k = {}
    last_flush_ac = 0
    for t in range(32):
        win, sl = t // 4, t % 4
        xw = []
        for h in range(2):
            if t > 0:
                wait("sp", "dv", norm_done)
            xw.append(dma("sp", f"xh{h}", xh[:, h, :],
                          dx_half[t * 128 : (t + 1) * 128, h * 2048 : (h + 1) * 2048]))
        vacl = 0
        for h in range(2):
            waitv("dve", xw[h])
            plain("dve", lambda v, h=h: v.tensor_scalar(
                sqdump[:, :], xh[:, h, :], 1.0, None, ALU.mult, ALU.add,
                accum_out=stats[:, 0 + h : 1 + h]))
            vacl = tag("dve", lambda v, h=h: v.scalar_tensor_tensor(
                sqdump[:, :], xh[:, h, :], 1.0, xh[:, h, :], ALU.mult, ALU.mult,
                accum_out=stats[:, 2 + h : 3 + h]))
        wait("dve", "dv", vacl)
        v1 = tag("dve", lambda v: v.tensor_add(stats[:, 4:5], stats[:, 0:1], stats[:, 1:2]))
        wait("dve", "dv", v1)
        plain("dve", lambda v: v.tensor_add(stats[:, 5:6], stats[:, 2:3], stats[:, 3:4]))
        v2 = tag("dve", lambda v: v.tensor_scalar(stats[:, 4:5], stats[:, 4:5], 1.0 / E, None, ALU.mult))
        wait("dve", "dv", v2)
        v3 = tag("dve", lambda v: v.tensor_mul(stats[:, 8:9], stats[:, 4:5], stats[:, 4:5]))
        wait("dve", "dv", v3)
        v4 = tag("dve", lambda v: v.tensor_scalar(
            stats[:, 5:6], stats[:, 5:6], 1.0 / E, stats[:, 8:9], ALU.mult, ALU.subtract))
        wait("act", "dv", max(v4, v_c0))
        va = tag("act", lambda s: s.activation(stats[:, 6:7], stats[:, 5:6], AF.Sqrt, bias=LN_EPS))
        wait("dve", "ac", va)
        v5 = tag("dve", lambda v: v.reciprocal(stats[:, 7:8], stats[:, 6:7]))
        wait("dve", "dv", v5)
        if t > 0:
            wait("dve", "pe", last_tp)
        for h in range(2):
            vn = tag("dve", lambda v, h=h: v.tensor_scalar(
                xnorm[:, h * 2048 : (h + 1) * 2048], xh[:, h, :], stats[:, 4:5], stats[:, 7:8],
                ALU.subtract, ALU.mult))
        vnorm = vn
        wait("pe", "dv", vnorm)
        if t == 0:
            waitv("pe", su_all)
        for kb in range(0, 32, 4):
            pbank = 6 + (kb // 4) % 2
            if (t, kb - 8) in ev_of_k:
                wait("pe", "ac", ev_of_k[(t, kb - 8)])
            elif t > 0 and kb < 8:
                wait("pe", "ac", ev_of_k[(t - 1, 24 + kb)])
            for kk in range(4):
                k = kb + kk
                vtp = tag("pe", lambda tt, k=k, kk=kk, pbank=pbank: tt.matmul(
                    ps[pbank][:, kk * 128 : kk * 128 + 128], xnorm[:, k * 128 : (k + 1) * 128],
                    ident[:, :], start=True, stop=True))
            wait("act", "pe", vtp)
            if len(fm_dma) > 0 and kb == 0:
                waitv("act", fm_dma[-1])
            ev_of_k[(t, kb)] = tag("act", lambda s, pbank=pbank, kb=kb, sl=sl: s.copy(
                fmstage.ap().rearrange("p (k c) -> p k c", k=32)[:, kb : kb + 4, sl * 128 : sl * 128 + 128],
                ps[pbank].ap().rearrange("p (k c) -> p k c", k=4)))
        last_tp = cnt["pe"]
        norm_done = cnt["dv"]
        if sl == 3:
            wait("pool", "ac", cnt["ac"])

            fm_dma.append(dma("pool", "fl",
                dxn_loc.ap().rearrange("(k p) t -> p k t", p=128)[:, :, win * 512 : (win + 1) * 512],
                fmstage.ap().rearrange("p (k t) -> p k t", k=32)))

    waitv("pool", gtot("fl"))
    wait("pool", "dv", v_c0)
    GR["cpx"] = GR.get("cpx", 0) + 16

    def cpx1(g):
        pv = pvals(g)
        g.dma_start(out=dxn.ap()[DynSlice(pv["par"], 1), :, :],
                    in_=dxn_loc.ap()).then_inc(sems["cpx"], 16)

    em("pool", cpx1)
    waitv("pool", gtot("cpx"))

    def pub(word):
        def fn(g, word=word):
            pv = pvals(g)
            g.dma_start(out=dflg.ap()[DynSlice(pv["par"], 1), word * 16 : word * 16 + 16],
                        in_=flag_sb[:, :]).then_inc(sems["pub"], 16)
        GR["pub"] = GR.get("pub", 0) + 16
        em("pool", fn)
        waitv("pool", gtot("pub"))

    def poll(word, psem_name):
        def fn(g, word=word, psem_name=psem_name):
            pv = pvals(g)
            with g.register("fv" + psem_name) as fv, g.register("tgt" + psem_name) as tgt:
                g.reg_mov(fv, 0)
                g.reg_mov(tgt, 0)
                with g.While(lambda: g.compute_val(fv) != MAGIC):
                    g.reg_add(tgt, tgt, 16)
                    g.dma_start(out=pollbuf[:, :],
                                in_=dflg.ap()[DynSlice(pv["oth"], 1), word * 16 : word * 16 + 16]
                                ).then_inc(sems[psem_name], 16)
                    g.wait_ge(sems[psem_name], tgt)
                    g.reg_load(fv, pollbuf[0:1, 0:1])
        em("pool", fn)

    if PH < 1:
        touch_out_and_stop()
        return _emit(nc, sems, streams, ps, GR)
    pub(0)
    poll(0, "p1")
    gat_all = None
    v_xnrdy = tag("pool", lambda g: g.nop(), "pl")
    barrier()

    if PH < 2:
        touch_out_and_stop()
        return _emit(nc, sems, streams, ps, GR)
    kv_ev = {}
    pass_last_pe = 0
    for p_ in range(2 if KSUB >= 2 else 1):
        strw = []
        for c in range(4):
            if p_ == 1:
                wait("pool", "pe", pass_last_pe)
            strw.append(dma("pool", "kst",
                kvstr.ap().rearrange("p (c k e) -> p c k e", c=4, k=32)[:, c, :, :],
                dw_kv[:, p_ * 512 + c * 128 : p_ * 512 + (c + 1) * 128].rearrange("(k p) c -> p k c", p=128)))
        if p_ == 0:
            for j in range(8):
                def gat(g, j=j):
                    pv = pvals(g)
                    g.dma_start(out=dxn_own.ap()[:, j * 128 : (j + 1) * 128],
                                in_=dxn.ap()[j // 4][:, DynSlice(pv["p128"] + (j % 4) * 1024, 128)]).then_inc(sems["gat"], 16)
                GR["gat"] = GR.get("gat", 0) + 16
                em("pool", gat)
            gat_all = gtot("gat")
        for tb in range(32):
            par2 = tb % 2
            if tb >= 2:
                wait("sp", "pe", kv_ev[("mmlast", p_, tb - 2)])
            xv = dma("sp", f"xb{par2}", xn_blk[:, par2, :].rearrange("p (k t) -> p k t", k=32),
                     dxn.ap()[tb // 16].rearrange("(k p) t -> p k t", p=128)[:, :, (tb % 16) * 256 : (tb % 16 + 1) * 256])
            if p_ == 0:
                if tb >= 2:
                    sn2, vv2 = kv_ev[(p_, tb - 2, 3)]
                    wait("sp", sn2, vv2)
                dma("sp", f"cs{par2}", cosblk[:, par2 * 256 : (par2 + 1) * 256], dcos_all[:, tb * 256 : (tb + 1) * 256])
                cv = dma("sp", f"cs{par2}", sinblk[:, par2 * 256 : (par2 + 1) * 256], dsin_all[:, tb * 256 : (tb + 1) * 256])
            waitv("pe", xv)
            if tb == 0:
                waitv("pe", strw[3])
                wait("pe", "pl", v_xnrdy)
            for c in range(4):
                if (p_, tb - 1, c) in kv_ev:
                    sn, vv = kv_ev[(p_, tb - 1, c)]
                    wait("pe", sn, vv)
                for e in range(31):
                    plain("pe", lambda tt, c=c, e=e, par2=par2, st=(e == 0):
                          tt.matmul(ps[c][:, 0:256],
                                    kvstr[:, (c * 32 + e) * 128 : (c * 32 + e) * 128 + 128],
                                    xn_blk[:, par2, e * 256 : (e + 1) * 256],
                                    start=st, stop=False))
                vmm = tag("pe", lambda tt, c=c, par2=par2: tt.matmul(
                    ps[c][:, 0:256], kvstr[:, (c * 32 + 31) * 128 : (c * 32 + 31) * 128 + 128],
                    xn_blk[:, par2, 31 * 256 : 32 * 256], start=False, stop=True))
                if c == 3:
                    kv_ev[("mmlast", p_, tb)] = vmm
                if KSUB == 11:
                    kv_ev[(p_, tb, c)] = ("pe", vmm)
                    continue
                if p_ == 0 and KSUB == 12:
                    wait("act", "pe", vmm)
                    if ("w", p_, tb - 2) in kv_ev:
                        waitv("act", kv_ev[("w", p_, tb - 2)])
                    va = tag("act", lambda s_, c=c, par2=par2: s_.copy(kvstage[:, par2, c * 256 : (c + 1) * 256], ps[c][:, 0:256]))
                    kv_ev[(p_, tb, c)] = ("ac", va)
                elif p_ == 0:
                    wait("dve", "pe", vmm)
                    waitv("dve", cv)
                    if ("w", p_, tb - 2) in kv_ev:
                        waitv("dve", kv_ev[("w", p_, tb - 2)])
                    plain("dve", lambda v, c=c, par2=par2: v.tensor_mul(rtmp1[:, 0:256], ps[c][0:64, 0:256], cosblk[:, par2 * 256 : par2 * 256 + 256]))
                    plain("dve", lambda v, c=c, par2=par2: v.tensor_mul(rtmp2[:, 0:256], ps[c][64:128, 0:256], sinblk[:, par2 * 256 : par2 * 256 + 256]))
                    tag("dve", lambda v, c=c, par2=par2: v.tensor_sub(kvstage[0:64, par2, c * 256 : (c + 1) * 256], rtmp1[:, 0:256], rtmp2[:, 0:256]))
                    plain("dve", lambda v, c=c, par2=par2: v.tensor_mul(rtmp1[:, 0:256], ps[c][64:128, 0:256], cosblk[:, par2 * 256 : par2 * 256 + 256]))
                    plain("dve", lambda v, c=c, par2=par2: v.tensor_mul(rtmp2[:, 0:256], ps[c][0:64, 0:256], sinblk[:, par2 * 256 : par2 * 256 + 256]))
                    vr = tag("dve", lambda v, c=c, par2=par2: v.tensor_add(kvstage[64:128, par2, c * 256 : (c + 1) * 256], rtmp1[:, 0:256], rtmp2[:, 0:256]))
                    kv_ev[(p_, tb, c)] = ("dv", vr)
                else:
                    wait("act", "pe", vmm)
                    if ("w", p_, tb - 2) in kv_ev:
                        waitv("act", kv_ev[("w", p_, tb - 2)])
                    va = tag("act", lambda s_, c=c, par2=par2: s_.copy(kvstage[:, par2, c * 256 : (c + 1) * 256], ps[c][:, 0:256]))
                    kv_ev[(p_, tb, c)] = ("ac", va)
            if KSUB == 11 or KSUB == 13:
                continue
            sn_l, vl = kv_ev[(p_, tb, 3)]
            wait("pool", sn_l, vl)
            if p_ == 0:
                sn0, vl0 = kv_ev[(p_, tb, 0)]
                wait("pool", sn0, vl0)
            kv_ev[("w", p_, tb)] = dma("pool", f"kw{par2}",
                dkv_loc.ap()[p_ * 512 : (p_ + 1) * 512, :].rearrange(
                    "(h p) t -> p h t", p=128)[:, :, tb * 256 : (tb + 1) * 256],
                kvstage.ap()[:, par2, :].rearrange("p (h t) -> p h t", h=4))
        pass_last_pe = kv_ev[("mmlast", p_, 31)]
        waitv("pool", gtot("kw0"))
        waitv("pool", gtot("kw1"))
        if KSUB >= 3:
            GR["cpk"] = GR.get("cpk", 0) + 16

            def cp_kv_p(g, p_=p_):
                pv = pvals(g)
                g.dma_start(out=dKV.ap()[DynSlice(pv["par"], 1), p_ * 512 : (p_ + 1) * 512, :],
                            in_=dkv_loc.ap()[p_ * 512 : (p_ + 1) * 512, :]).then_inc(sems["cpk"], 16)

            em("pool", cp_kv_p)

    if KSUB >= 3:
        waitv("pool", gtot("cpk"))

    if KSUB >= 4:
        pub(1)
        poll(1, "p2")
    v_kvrdy = tag("pool", lambda g: g.nop(), "pl")
    barrier()

    if KSUB < 9:
        touch_out_and_stop()
        return _emit(nc, sems, streams, ps, GR)
    if PH < 3:
        touch_out_and_stop()
        return _emit(nc, sems, streams, ps, GR)
    waitv("sp", gat_all)
    v_xnown = dma("sp", "xo", xn_own.ap().rearrange("p (k c) -> p k c", k=32),
                  dxn_own.ap().rearrange("(k p) c -> p k c", p=128))

    rs_pl = {}

    def emit_sweep(k):
        pr = k % 2
        if k >= 2:
            wait("sp", "dv", rs_pl[k - 2])
        n0 = HALF - RQ0
        dma("sp", f"rs{pr}", rsw_in[:, pr, 0:n0], dxn.ap()[0, k * 128 : (k + 1) * 128, RQ0:HALF])
        rw_ = dma("sp", f"rs{pr}", rsw_in[:, pr, n0:RCNT], dxn.ap()[1, k * 128 : (k + 1) * 128, 0 : RQ1 - HALF])
        waitv("dve", rw_)
        vac = tag("dve", lambda v, pr=pr: v.reduce_sum(rpart[:, pr : pr + 1], rsw_in[:, pr, 0:RCNT], axis=AX))
        wait("dve", "dv", vac)
        vadd = tag("dve", lambda v, k=k, pr=pr: v.tensor_add(racc[:, k : k + 1], racc[:, k : k + 1], rpart[:, pr : pr + 1]))
        rs_pl[k] = vadd

    q_dma_w = []
    rope_dv = {}
    qmm_of = {}
    for qh in range(NQH):
        pq = qh % 2
        if qh >= 2:
            wait("pool", "pe", qmm_of[qh - 2])
        sw = dma("pool", f"qs{pq}", qstr[:, pq, :].rearrange("p (k c) -> p k c", k=32),
                 dw_q[:, qh * 128 : (qh + 1) * 128].rearrange("(k p) c -> p k c", p=128))
        for tb in range(2):
            bank = tb
            waitv("pe", sw)
            if qh == 0 and tb == 0:
                wait("pe", v_xnown[0], v_xnown[1])
            if (qh - 1, tb) in rope_dv:
                wait("pe", "dv", rope_dv[(qh - 1, tb)])
            for e in range(31):
                plain("pe", lambda tt, pq=pq, e=e, tb=tb, bank=bank, st=(e == 0):
                      tt.matmul(ps[bank][:, 0:512],
                                qstr[:, pq, e * 128 : (e + 1) * 128],
                                xn_own[:, e * 1024 + tb * 512 : e * 1024 + tb * 512 + 512],
                                start=st, stop=False))
            vmm = tag("pe", lambda tt, pq=pq, tb=tb, bank=bank: tt.matmul(
                ps[bank][:, 0:512], qstr[:, pq, 31 * 128 : 32 * 128],
                xn_own[:, 31 * 1024 + tb * 512 : 31 * 1024 + tb * 512 + 512], start=False, stop=True))
            if tb == 1:
                qmm_of[qh] = vmm
            wait("dve", "pe", vmm)
            if qh == 0 and tb == 0:
                waitv("dve", su_all)
            if qh >= 1:
                waitv("dve", q_dma_w[-1])
            plain("dve", lambda v, bank=bank, tb=tb: v.tensor_mul(rtmp1[:, :], ps[bank][0:64, 0:512], cos_own[:, tb * 512 : (tb + 1) * 512]))
            plain("dve", lambda v, bank=bank, tb=tb: v.tensor_mul(rtmp2[:, :], ps[bank][64:128, 0:512], sin_own[:, tb * 512 : (tb + 1) * 512]))
            tag("dve", lambda v, tb=tb: v.tensor_sub(qstage[0:64, tb * 512 : (tb + 1) * 512], rtmp1[:, :], rtmp2[:, :]))
            plain("dve", lambda v, bank=bank, tb=tb: v.tensor_mul(rtmp1[:, :], ps[bank][64:128, 0:512], cos_own[:, tb * 512 : (tb + 1) * 512]))
            plain("dve", lambda v, bank=bank, tb=tb: v.tensor_mul(rtmp2[:, :], ps[bank][0:64, 0:512], sin_own[:, tb * 512 : (tb + 1) * 512]))
            v2 = tag("dve", lambda v, tb=tb: v.tensor_add(qstage[64:128, tb * 512 : (tb + 1) * 512], rtmp1[:, :], rtmp2[:, :]))
            rope_dv[(qh, tb)] = v2
        wait("sp", "dv", rope_dv[(qh, 1)])
        qw = dma("sp", "qw", dQ.ap()[qh, :, :], qstage[:, :])
        q_dma_w.append(qw)
        emit_sweep(qh)
    v_qdone = q_dma_w[-1]
    v_sweep = cnt["dv"]
    barrier()

    if PH < 4:
        touch_out_and_stop()
        return _emit(nc, sems, streams, ps, GR)
    v_vpones = cnt["dv"]
    ol_guard = {}
    gend = {}
    kvq = {}

    def b_loads(x):
        px = x % 2
        krow = (x % 4) * 128
        vrow = 512 + (x % 4) * 128
        kvq[x] = (
            dma("sp", f"kg{px}", kg[:, px, :], dKV.ap()[x // 4, krow : krow + 128, :]),
            dma("sp", f"vf{px}", vfm[:, px, :], dKV.ap()[x // 4, vrow : vrow + 128, :]),
            dma("sp", f"qg{px}", qg[:, px, :].rearrange("p (h t) -> p h t", h=4),
                dQ.ap()[4 * x : 4 * x + 4, :, :].rearrange("h p t -> p h t")),
        )

    for g_ in range(NKV):
        pg = g_ % 2
        if g_ == 0:
            waitv("sp", gtot("qw"))
            b_loads(0)
        if g_ + 1 < NKV:
            if g_ >= 1:
                wait("sp", "pe", gend[g_ - 1])
            b_loads(g_ + 1)
        kw_, vw_, qgw = kvq[g_]
        waitv("pe", vw_)
        if g_ == 0:
            wait("pe", "pl", v_kvrdy)
            wait("pe", "dv", v_vpones)
        if g_ >= 1:
            wait("pe", "pe", gend[g_ - 1])
        vev = {}
        for b4 in range(0, 64, 4):
            pbank = (b4 // 4) % 4
            if b4 >= 16:
                wait("pe", "ac", vev[b4 - 16])
            if b4 == 8 and "lb" in ol_guard:
                wait("pe", "ac", ol_guard["lb"])
            for kk in range(4):
                vtp = tag("pe", lambda tt, b=b4 + kk, kk=kk, pg=pg, pbank=pbank: tt.matmul(
                    ps[pbank][:, kk * 128 : kk * 128 + 128], vfm[:, pg, b * 128 : (b + 1) * 128], ident[:, :], start=True, stop=True))
            wait("act", "pe", vtp)
            vev[b4] = tag("act", lambda s_, b4=b4, pbank=pbank: s_.copy(
                vp[:, b4 * 128 : b4 * 128 + 512], ps[pbank][:, 0:512]))
        v_vp = vev[60]
        waitv("pe", kw_)
        waitv("pe", qgw)
        wait("pe", "ac", v_vp)
        for qhl in range(4):
            qh = 4 * g_ + qhl
            for grp in range(2):
                j0 = grp * 4
                kmax = 32 if grp == 0 else 64
                batches = []
                m0 = 0
                while m0 < kmax:
                    batches.append(list(range(m0, min(m0 + 3, kmax))))
                    m0 += 3
                expB = {}
                maskB = {}
                pvB = {}
                addB = {}

                def emit_Sbatch(bi):
                    ms = batches[bi]
                    if bi >= 2:
                        wait("pe", "ac", expB[bi - 2])
                    if bi == 0 and "lb" in ol_guard:
                        wait("pe", "ac", ol_guard["lb"])
                    vS = None
                    for m in ms:
                        slot = (bi % 2) * 3 + (m - ms[0])
                        jlo = max(j0, m // 8)
                        rel = jlo - j0
                        ncols = (j0 + 4 - jlo) * 128
                        fn = (lambda tt, slot=slot, m=m, rel=rel, ncols=ncols, pg=pg, qhl=qhl, j0=j0:
                              tt.matmul(ps[slot][:, rel * 128 : rel * 128 + ncols],
                                        kg[:, pg, m * 128 : (m + 1) * 128],
                                        qg[:, pg, qhl * 1024 + j0 * 128 + rel * 128 : qhl * 1024 + j0 * 128 + rel * 128 + ncols],
                                        start=True, stop=True))
                        if m == ms[-1]:
                            vS = tag("pe", fn)
                        else:
                            plain("pe", fn)
                    wait("act", "pe", vS)
                    if bi >= 2:
                        wait("act", "pe", pvB[bi - 2])
                        wait("act", "dv", addB[bi - 2])
                    vE = None
                    for m in ms:
                        slot = (bi % 2) * 3 + (m - ms[0])
                        jlo = max(j0, m // 8)
                        rel = jlo - j0
                        ncols = (j0 + 4 - jlo) * 128
                        vE = tag("act", lambda s_, slot=slot, pp=(bi % 2), mi=(m - ms[0]), rel=rel, ncols=ncols: s_.activation(
                            ptbP[pp][:, mi * 512 + rel * 128 : mi * 512 + rel * 128 + ncols],
                            ps[slot][:, rel * 128 : rel * 128 + ncols],
                            AF.Exp, bias=EXP_BIAS, scale=QK_SCALE))
                    expB[bi] = vE
                    vM = None
                    for m in ms:
                        j_d = m // 8
                        if j0 <= j_d < j0 + 4:
                            if vM is None:
                                wait("dve", "ac", vE)
                            vM = tag("dve", lambda v, pp=(bi % 2), mi=(m - ms[0]), j_d=j_d, m=m, j0=j0: v.tensor_mul(
                                ptbP[pp][:, mi * 512 + (j_d - j0) * 128 : mi * 512 + (j_d - j0) * 128 + 128],
                                ptbP[pp][:, mi * 512 + (j_d - j0) * 128 : mi * 512 + (j_d - j0) * 128 + 128],
                                octm[:, (m % 8) * 128 : (m % 8) * 128 + 128]))
                    maskB[bi] = vM
                    if vM is None:
                        wait("dve", "ac", vE)
                    vA = None
                    for m in ms:
                        mi = m - ms[0]
                        jlo = max(j0, m // 8)
                        rel = jlo - j0
                        if m == 0 and "lmm" in ol_guard:
                            wait("dve", "pe", ol_guard["lmm"])
                        for hh in range(2):
                            lo = max(rel * 128, hh * 256)
                            hi = (hh + 1) * 256
                            if lo >= hi:
                                continue
                            if m == 0:
                                vA = tag("dve", lambda v, pp=(bi % 2), mi=mi, hh=hh, lo=lo, hi=hi: v.tensor_copy(
                                    accP[hh][:, lo - hh * 256 : hi - hh * 256],
                                    ptbP[pp][:, mi * 512 + lo : mi * 512 + hi]))
                            else:
                                vA = tag("dve", lambda v, pp=(bi % 2), mi=mi, hh=hh, lo=lo, hi=hi: v.tensor_add(
                                    accP[hh][:, lo - hh * 256 : hi - hh * 256],
                                    accP[hh][:, lo - hh * 256 : hi - hh * 256],
                                    ptbP[pp][:, mi * 512 + lo : mi * 512 + hi]))
                    addB[bi] = vA

                def emit_PVbatch(bi):
                    ms = batches[bi]
                    wait("pe", "ac", expB[bi])
                    if maskB[bi] is not None:
                        wait("pe", "dv", maskB[bi])
                    if bi == 0:
                        if "O" in ol_guard:
                            wait("pe", "dv", ol_guard["O"])
                    vl = None
                    for m in ms:
                        jlo = max(j0, m // 8)
                        rel = jlo - j0
                        ncols = (j0 + 4 - jlo) * 128
                        fn = (lambda tt, pp=(bi % 2), mi=(m - ms[0]), m=m, rel=rel, ncols=ncols, st=(m == 0):
                              tt.matmul(ps[6][:, rel * 128 : rel * 128 + ncols],
                                        vp[:, m * 128 : m * 128 + 128],
                                        ptbP[pp][:, mi * 512 + rel * 128 : mi * 512 + rel * 128 + ncols],
                                        start=st, stop=(m == kmax - 1)))
                        if m == ms[-1]:
                            vl = tag("pe", fn)
                        else:
                            plain("pe", fn)
                    pvB[bi] = vl

                emit_Sbatch(0)
                emit_Sbatch(1)
                for bi in range(2, len(batches)):
                    emit_PVbatch(bi - 2)
                    emit_Sbatch(bi)
                emit_PVbatch(len(batches) - 2)
                emit_PVbatch(len(batches) - 1)

                wait("pe", "dv", addB[len(batches) - 1])
                if "l" in ol_guard:
                    wait("pe", "dv", ol_guard["l"])
                plain("pe", lambda tt: tt.matmul(ps[7][0:1, 0:256], ones_cf[:, :], accP[0][:, :], start=True, stop=True))
                v_lmm = tag("pe", lambda tt: tt.matmul(ps[7][0:1, 256:512], ones_cf[:, :], accP[1][:, :], start=True, stop=True))
                ol_guard["lmm"] = v_lmm
                wait("dve", "pe", v_lmm)
                vrc0 = tag("dve", lambda v: v.reciprocal(rl_f[0:1, :], ps[7][0:1, 0:512]))
                ol_guard["l"] = vrc0
                wait("dve", "dv", vrc0)
                vrc = tag("dve", lambda v: v.tensor_copy(rl_sb[0:1, :], rl_f[0:1, :]))
                wait("pe", "dv", vrc)
                vbc = tag("pe", lambda tt: tt.matmul(ps[2][:, 0:512], ones_r[0:1, :], rl_sb[0:1, :], start=True, stop=True))
                wait("act", "pe", vbc)
                vlb = tag("act", lambda s_: s_.copy(lb_sb[:, :], ps[2][:, 0:512]))
                ol_guard["lb"] = vlb
                wait("dve", "ac", vlb)
                vtt = tag("dve", lambda v, qh=qh, grp=grp: v.tensor_mul(
                    attn_fm[:, qh * 1024 + grp * 512 : qh * 1024 + grp * 512 + 512],
                    ps[6][:, 0:512], lb_sb[:, :]))
                ol_guard["O"] = vtt
        gend[g_] = cnt["pe"]
    v_attn_done_dv = cnt["dv"]
    barrier()

    if PH < 5:
        touch_out_and_stop()
        return _emit(nc, sems, streams, ps, GR)
    c_pe_of = {}
    c_ev = {}
    for o in range(16):
        po = o % 2
        if o >= 2:
            wait("pool", "pe", c_pe_of[o - 2])
        ww = dma("pool", f"wc{po}", wslab[:, po, :].rearrange("p (k t) -> p k t", k=32),
                 dw_o[:, o * 256 : (o + 1) * 256].rearrange("(k p) t -> p k t", p=128))
        waitv("pe", ww)
        if o == 0:
            wait("pe", "dv", v_attn_done_dv)
        if o >= 1:
            wait("pe", "ac", c_ev[o - 1])
        for c in range(32):
            for t in range(8):
                mmfn = (lambda tt, c=c, t=t, po=po, st=(c == 0), sp_=(c == 31):
                        tt.matmul(ps[t][:, 0:256],
                                  attn_fm[:, c * 1024 + t * 128 : c * 1024 + t * 128 + 128],
                                  wslab[:, po, c * 256 : (c + 1) * 256],
                                  start=st, stop=sp_))
                if c == 31 and t == 7:
                    vmm = tag("pe", mmfn)
                else:
                    plain("pe", mmfn)
        c_pe_of[o] = vmm
        wait("act", "pe", vmm)
        if o >= 2:
            waitv("act", c_ev[("sdma", o - 2)])
        for t in range(8):
            vev2 = tag("act", lambda s_, t=t, po=po: s_.copy(sstage[:, po, t * 256 : (t + 1) * 256], ps[t][:, 0:256]))
        c_ev[o] = vev2
        wait("sp", "ac", vev2)
        sd = dma("sp", f"sd{po}", dsS.ap()[:, o * 256 : (o + 1) * 256].rearrange("(t p) c -> p t c", p=128),
                 sstage.ap()[:, po, :].rearrange("p (t c) -> p t c", t=8))
        c_ev[("sdma", o)] = sd
    v_s_done = c_ev[("sdma", 15)]
    if KDBG:
        wait("sp", "dv", v_attn_done_dv)
        dma("sp", "dbg", dbg_attn, attn_fm[:, :])
    barrier()

    if PH < 6:
        touch_out_and_stop()
        return _emit(nc, sems, streams, ps, GR)
    waitv("pool", gtot("sd0"))
    waitv("pool", gtot("sd1"))
    dma("pool", "dcon", a_bf.ap().rearrange("p (k c) -> p k c", k=32), dA.rearrange("(k p) c -> p k c", p=128))
    dma("pool", "dcon", b_bf[:, :], dB)
    dma("pool", "dcon", fb_bf[:, :], dbias)
    dcon_all = gtot("dcon")
    v_xo2 = dma("sp", "xo", xn_own.ap().rearrange("p (k c) -> p k c", k=32),
                dxn_own.ap().rearrange("(k p) c -> p k c", p=128))

    vqbf = tag("dve", lambda v: v.tensor_scalar(q_bf[:, :], racc[:, :], 1.0 / RCNT, None, ALU.mult))
    wait("pe", "dv", vqbf)
    waitv("pe", suP_all)
    for k in range(32):
        plain("pe", lambda tt, k=k, st=(k == 0): tt.matmul(
            ps[0][0:1, 0:8], q_bf[:, k : k + 1], rw_bf[:, k * 8 : (k + 1) * 8], start=st, stop=False))
    waitv("pe", su_all)
    vlg = tag("pe", lambda tt: tt.matmul(ps[0][0:1, 0:8], ones_r[0:1, 0:1], rb_bf[0:1, :], start=False, stop=True))
    wait("dve", "pe", vlg)
    vc1 = tag("dve", lambda v: v.tensor_copy(rsm[0:1, 0:8], ps[0][0:1, 0:8]))
    wait("dve", "dv", vc1)
    vc2 = tag("dve", lambda v: v.reduce_max(rsm[0:1, 8:9], rsm[0:1, 0:8], axis=AX))
    wait("dve", "dv", vc2)
    vc3 = tag("dve", lambda v: v.tensor_scalar(rsm[0:1, 9:10], rsm[0:1, 8:9], -1.0, None, ALU.mult))
    wait("act", "dv", vc3)
    vc4 = tag("act", lambda s_: s_.activation(rsm[0:1, 16:24], rsm[0:1, 0:8], AF.Exp, bias=rsm[0:1, 9:10]))
    wait("dve", "ac", vc4)
    vc5 = tag("dve", lambda v: v.reduce_sum(rsm[0:1, 10:11], rsm[0:1, 16:24], axis=AX))
    wait("dve", "dv", vc5)
    vc6 = tag("dve", lambda v: v.reciprocal(rsm[0:1, 11:12], rsm[0:1, 10:11]))
    wait("dve", "dv", vc6)
    vc7 = tag("dve", lambda v: v.tensor_scalar(rsm_bf[0:1, 0:8], rsm[0:1, 16:24], rsm[0:1, 11:12], 2.0, ALU.mult, ALU.mult))
    wait("pe", "dv", vc7)
    vc8 = tag("pe", lambda tt: tt.matmul(ps[1][0:8, 0:1], rsm_bf[0:1, 0:8], ones_r[0:1, 0:1], start=True, stop=True))
    wait("dve", "pe", vc8)
    vc9 = tag("dve", lambda v: v.tensor_copy(rcol_bf[:, :], ps[1][0:8, 0:1]))
    wait("pe", "dv", vc9)
    waitv("pe", su_all)
    vc10 = tag("pe", lambda tt: tt.matmul(ps[2][:, 0:1], emat[:, :], rcol_bf[:, :], start=True, stop=True))
    wait("dve", "pe", vc10)
    v_rer = tag("dve", lambda v: v.tensor_copy(r_er[:, :], ps[2][:, 0:1]))

    tl_g = {}
    for t in range(8):
        b1 = 3 + (t % 2)
        if t == 0:
            waitv("pe", v_xo2)
            waitv("pe", dcon_all)
        if ("ev", t - 2) in tl_g:
            wait("pe", "ac", tl_g[("ev", t - 2)])
        for e in range(31):
            plain("pe", lambda tt, e=e, t=t, b1=b1, st=(e == 0):
                  tt.matmul(ps[b1][:, 0:128],
                            xn_own[:, e * 1024 + t * 128 : e * 1024 + t * 128 + 128],
                            a_bf[:, e * 128 : (e + 1) * 128], start=st, stop=False))
        vmm = tag("pe", lambda tt, t=t, b1=b1: tt.matmul(
            ps[b1][:, 0:128], xn_own[:, 31 * 1024 + t * 128 : 31 * 1024 + t * 128 + 128],
            a_bf[:, 31 * 128 : 32 * 128], start=False, stop=True))
        wait("act", "pe", vmm)
        am = t % 2
        if ("tp", t - 2) in tl_g:
            wait("act", "pe", tl_g[("tp", t - 2)])
        vev3 = tag("act", lambda s_, b1=b1, am=am: s_.copy(tltmp[:, am * 128 : am * 128 + 128], ps[b1][:, 0:128]))
        tl_g[("ev", t)] = vev3
        wait("pe", "ac", vev3)
        b2 = 5 + (t % 2)
        if ("sc", t - 2) in tl_g:
            wait("pe", "dv", tl_g[("sc", t - 2)])
        vtp = tag("pe", lambda tt, am=am, b2=b2: tt.matmul(
            ps[b2][:, 0:128], tltmp[:, am * 128 : am * 128 + 128], ident[:, :], start=True, stop=True))
        tl_g[("tp", t)] = vtp
        wait("dve", "pe", vtp)
        if t == 0:
            wait("dve", "dv", v_rer)
        vsc = tag("dve", lambda v, b2=b2, t=t: v.tensor_scalar(
            t_lT[:, t * 128 : (t + 1) * 128], ps[b2][:, 0:128], r_er[:, 0:1], None, ALU.mult))
        tl_g[("sc", t)] = vsc
    v_tl_done = cnt["dv"]

    d_pe_of = {}
    sil_of = {}
    sg_of = {}
    out_dmas = []
    xep_guard = {}
    li = 0
    for o in range(16):
        po = o % 2
        if o >= 2:
            wait("pool", "pe", d_pe_of[o - 2])
            wait("sp", "dv", sg_of[(o - 2, 7)])
        ww = dma("pool", f"wc{po}", wslab[:, po, :].rearrange("p (k t) -> p k t", k=32),
                 dw_f[:, o * 256 : (o + 1) * 256].rearrange("(k p) t -> p k t", p=128))
        sv = dma("sp", f"sv{po}", sdb[:, po, :].rearrange("p (t c) -> p t c", t=8),
                 dsS.ap()[:, o * 256 : (o + 1) * 256].rearrange("(t p) c -> p t c", p=128))
        for t in range(8):
            waitv("pe", ww)
            if o == 0 and t == 0:
                waitv("pe", dcon_all)
                wait("pe", "dv", v_tl_done)
            if (o - 1, t) in sil_of:
                wait("pe", "ac", sil_of[(o - 1, t)])
            for e in range(32):
                plain("pe", lambda tt, e=e, t=t, po=po, st=(e == 0):
                      tt.matmul(ps[t][:, 0:256],
                                xn_own[:, e * 1024 + t * 128 : e * 1024 + t * 128 + 128],
                                wslab[:, po, e * 256 : (e + 1) * 256], start=st, stop=False))
            plain("pe", lambda tt, t=t, o=o: tt.matmul(
                ps[t][:, 0:256], ones_r[0:1, :], fb_bf[0:1, o * 256 : (o + 1) * 256], start=False, stop=False))
            vmm = tag("pe", lambda tt, t=t, o=o: tt.matmul(
                ps[t][:, 0:256], t_lT[:, t * 128 : (t + 1) * 128], b_bf[:, o * 256 : (o + 1) * 256], start=False, stop=True))
            if t == 7:
                d_pe_of[o] = vmm
            pg2 = li % 2
            wait("act", "pe", vmm)
            if li >= 2:
                wait("act", "dv", sg_of[((li - 2) // 8, (li - 2) % 8)])
            vsi = tag("act", lambda s_, t=t, pg2=pg2: s_.activation(gtile[:, pg2, :], ps[t][:, 0:256], AF.Silu))
            sil_of[(o, t)] = vsi
            xv = dma("sp", f"xe{li % 2}", xep[:, li % 2, :], dx_own[t * 128 : (t + 1) * 128, o * 256 : (o + 1) * 256])
            wait("dve", "ac", vsi)
            waitv("dve", sv)
            if len(out_dmas) >= 2:
                waitv("dve", out_dmas[-2])
            vsg = tag("dve", lambda v, t=t, po=po, pg2=pg2, li=li: v.tensor_mul(
                outep[:, li % 2, :], sdb[:, po, t * 256 : (t + 1) * 256], gtile[:, pg2, :]))
            sg_of[(o, t)] = vsg
            waitv("dve", xv)
            wait("dve", "dv", vsg)
            vout = tag("dve", lambda v, li=li: v.tensor_add(outep[:, li % 2, :], outep[:, li % 2, :], xep[:, li % 2, :]))
            sg_of[(o, t)] = vout
            wait("sp", "dv", vout)
            ov = dma("sp", f"ov{li % 2}", dout[t * 128 : (t + 1) * 128, o * 256 : (o + 1) * 256], outep[:, li % 2, :])
            out_dmas.append(ov)
            li += 1

    if KDBG:
        wait("sp", "dv", cnt["dv"])
        wait("sp", "ac", cnt["ac"])
        dma("sp", "dbg", dbg_xn, xn_own[:, :])
        dma("sp", "dbg", dbg_q, dQ.ap()[0, :, :])
        dma("sp", "dbg", dbg_kv, dKV.ap().rearrange("a b t -> (a b) t")[:, 0:512])
        dma("sp", "dbg", dbg_s, dsS.ap()[:, :])
        dma("sp", "dbg", dbg_r, rsm[:, :])
        dma("sp", "dbg", dbg_tl, t_lT[:, :])
        dma("sp", "dbg", dbg_racc, racc[:, :])

    def clr(g):
        pv = pvals(g)
        g.dma_start(out=dflg.ap()[DynSlice(pv["oth"], 1), 0:32], in_=zero_sb[:, :]).then_inc(sems["clr"], 16)

    GR["clr"] = GR.get("clr", 0) + 16
    em("pool", clr)
    POOLG = ("suP", "fl", "kst", "kw0", "kw1", "cpx", "gat", "cpk", "pub",
             "qs0", "qs1", "wc0", "wc1", "dcon", "clr")
    for grp in list(GR):
        q = "pool" if grp in POOLG else "sp"
        wait(q, grp, GR[grp])

    return _emit(nc, sems, streams, ps, GR)


def _emit(nc, sems, streams, ps, GR):
    from contextlib import ExitStack
    with ExitStack() as ctx:
        e = ctx.enter_context
        for sname in ("pe", "ac", "dv", "pl", "bar", "p1", "p2"):
            sems[sname] = e(nc.semaphore(sname))
        for grp in GR:
            sems[grp] = e(nc.semaphore("g_" + grp))
        for i in range(8):
            ps[i] = e(nc.psum_tensor(f"ps{i}", [128, 512], F32))
        block = e(nc.Block())

        @block.tensor
        def _(t):
            for fn in streams["pe"]:
                fn(t)

        @block.scalar
        def _(s):
            for fn in streams["act"]:
                fn(s)

        @block.vector
        def _(v):
            for fn in streams["dve"]:
                fn(v)

        @block.gpsimd
        def _(g):
            for fn in streams["pool"]:
                fn(g)

        @block.sync
        def _(g):
            for fn in streams["sp"]:
                fn(g)

    return nc



def _prep_inputs(L0_attn_x, L0_attn_w_in, L0_attn_w_out, L0_ffn_W, L0_ffn_b,
                 L0_ffn_A, L0_ffn_B, L0_ffn_router_W, L0_ffn_router_b):
    x = np.asarray(L0_attn_x).reshape(S, E)
    w_in = np.asarray(L0_attn_w_in)
    w_out = np.asarray(L0_attn_w_out)
    W = np.asarray(L0_ffn_W)
    fb = np.ascontiguousarray(np.asarray(L0_ffn_b).reshape(1, E))
    A = np.asarray(L0_ffn_A)
    B = np.asarray(L0_ffn_B)
    rW = np.asarray(L0_ffn_router_W)
    rb = np.ascontiguousarray(np.asarray(L0_ffn_router_b).reshape(1, 8))

    w_qT = np.ascontiguousarray(w_in[0:4096].T)
    w_oT = np.ascontiguousarray(w_out.T)
    w_fT = np.ascontiguousarray(W.T)
    a_T = np.ascontiguousarray(A.transpose(2, 0, 1).reshape(E, 128))
    b_T = np.ascontiguousarray(B.transpose(0, 2, 1).reshape(128, E))
    rw_T = np.ascontiguousarray(rW.T)

    inv = 1.0 / (10000.0 ** (np.arange(0, 64, dtype=np.float64) * 2.0 / 128.0))
    pos = np.arange(S, dtype=np.float64)
    ang = pos[None, :] * inv[:, None]
    cos_all = np.ascontiguousarray(np.cos(ang).astype(np.float32))
    sin_all = np.ascontiguousarray(np.sin(ang).astype(np.float32))

    ident = np.eye(128, dtype=ml_dtypes.bfloat16)
    ones_r = np.ones((1, 128), dtype=ml_dtypes.bfloat16)
    emat = np.zeros((8, 128), dtype=np.float32)
    for e_ in range(8):
        emat[e_, e_ * 16 : (e_ + 1) * 16] = 1.0
    emat = emat.astype(ml_dtypes.bfloat16)

    in_maps = []
    for i in range(NCORE):
        par = i % 2
        own_rows = ((8 * np.arange(8)[:, None] + i) * 128 + np.arange(128)[None, :]).reshape(-1)
        oct_i = np.zeros((128, 8 * 128), dtype=np.float32)
        for o in range(8):
            if o < i:
                oct_i[:, o * 128 : (o + 1) * 128] = 1.0
            elif o == i:
                oct_i[:, o * 128 : (o + 1) * 128] = np.triu(np.ones((128, 128), dtype=np.float32)).T.T * 0 + np.triu(np.ones((128, 128), dtype=np.float32))
        oct_i = oct_i.astype(ml_dtypes.bfloat16)
        krows = w_in[4096 + par * 512 : 4096 + (par + 1) * 512]
        vrows = w_in[5120 + par * 512 : 5120 + (par + 1) * 512]
        w_kvT = np.ascontiguousarray(np.concatenate([krows, vrows], axis=0).T)
        in_maps.append({
            "x_half": np.ascontiguousarray(x[par * HALF : (par + 1) * HALF]),
            "x_own": np.ascontiguousarray(x[own_rows]),
            "w_qT": w_qT, "w_kvT": w_kvT, "w_oT": w_oT, "w_fT": w_fT,
            "a_T": a_T, "b_T": b_T, "fb": fb, "rw_T": rw_T, "rb": rb,
            "cos_own": np.ascontiguousarray(cos_all[:, own_rows]),
            "sin_own": np.ascontiguousarray(sin_all[:, own_rows]),
            "cos_all": cos_all, "sin_all": sin_all,
            "identity": ident, "octmask": oct_i, "onesrow": ones_r, "emat": emat,
        })
    return in_maps


def kernel(**inputs):
    in_maps = _prep_inputs(**inputs)
    nc = build_nc()
    res = run_bass_kernel_spmd(nc, in_maps, core_ids=list(range(NCORE)))
    out = np.zeros((S, E), dtype=np.float32)
    for i in range(NCORE):
        own_rows = ((8 * np.arange(8)[:, None] + i) * 128 + np.arange(128)[None, :]).reshape(-1)
        out[own_rows] = res.results[i]["out"]
    return out.reshape(1, S, E)
